# version 11
# baseline (speedup 1.0000x reference)
import os
import numpy as np
import ml_dtypes
import concourse.bass as bass
import concourse.mybir as mybir
from concourse.bass import IndirectOffsetOnAxis
from concourse.bass_utils import run_bass_kernel_spmd
from contextlib import ExitStack

F32 = mybir.dt.float32
BF16 = mybir.dt.bfloat16
I32 = mybir.dt.int32
F16 = mybir.dt.float16
AF = mybir.ActivationFunctionType
ALU = mybir.AluOpType
AX = mybir.AxisListType

T = 2048
C = 1024
S = 2048
NH = 16
E = 64
HF = 4096
NE = 8
CAP = 512
NBIS = 26
BIG = 30000.0

PHASES = int(os.environ.get("DEV_PHASES", "9"))
DEBUG = int(os.environ.get("DEV_DEBUG", "0"))

ENGS = ("sync", "tensor", "vector", "scalar", "gpsimd")
SEMS = ("qS", "qG", "tS", "vS", "sS")


class Env:
    def __init__(self, nc, ctx, future=None):
        self.nc = nc
        self.ctx = ctx
        self.h = {}
        self.c = {}
        self.marks = {}
        self.future = future
        for n in SEMS:
            self.get(n)

    def get(self, n):
        if n not in self.h:
            self.h[n] = self.ctx.enter_context(self.nc.semaphore(n))
            self.c[n] = 0
        return self.h[n]

    def inc(self, inst, name, k=1):
        inst.then_inc(self.get(name), k)
        self.c[name] += k

    def mark(self, name, sm):
        self.marks[name] = (sm, self.c[sm])

    def incm(self, inst, sm, mname, k=1):
        self.inc(inst, sm, k)
        self.mark(mname, sm)

    def dma(self, eng, q, out, in_, mname=None, **kw):
        inst = eng.dma_start(out=out, in_=in_, **kw)
        self.inc(inst, q, 16)
        if mname:
            self.mark(mname, q)

    def wait(self, eng, mname):
        if mname in self.marks:
            sm, v = self.marks[mname]
        elif self.future is not None:
            sm, v = self.future[mname]
        else:
            return
        if v > 0:
            eng.wait_ge(self.get(sm), v)

    def vself(self, eng, inst):
        self.inc(inst, "vS", 1)
        eng.wait_ge(self.h["vS"], self.c["vS"])

    def waitq(self, eng, q):
        if self.c.get(q, 0) > 0:
            eng.wait_ge(self.h[q], self.c[q])


ADDRS = {}

def build_nc(future=None):
    ADDRS.clear()
    nc = bass.Bass("TRN2", target_bir_lowering=False)
    dt = nc.dram_tensor

    x_tm = dt("x_tm", [T, C], F32, kind="ExternalInput").ap()
    xT_d = dt("xT", [C, T], F32, kind="ExternalInput").ap()
    srcT_d = dt("srcT", [C, S], F32, kind="ExternalInput").ap()
    valT_d = dt("valT", [C, S], F32, kind="ExternalInput").ap()
    Wq_d = dt("Wq", [C, C], F32, kind="ExternalInput").ap()
    Wk_d = dt("Wk", [C, C], F32, kind="ExternalInput").ap()
    Wv_d = dt("Wv", [C, C], F32, kind="ExternalInput").ap()
    Wo_d = dt("Wo", [C, C], F32, kind="ExternalInput").ap()
    biases_d = dt("biases", [1, 6 * C], F32, kind="ExternalInput").ap()
    gate_d = dt("gate_w", [C, NE], F32, kind="ExternalInput").ap()
    w1_d = dt("w1", [NE, C, HF], F32, kind="ExternalInput").ap()
    w2_d = dt("w2", [NE, HF, C], F32, kind="ExternalInput").ap()
    ident_d = dt("ident", [128, 128], BF16, kind="ExternalInput").ap()
    onesf_d = dt("onesf", [1, 128], F32, kind="ExternalInput").ap()
    onescf_d = dt("onescf", [128, 1], F32, kind="ExternalInput").ap()
    sel16_d = dt("sel16", [16, 16 * 128], BF16, kind="ExternalInput").ap()
    iota8_d = dt("iota8", [NE, T], F16, kind="ExternalInput").ap()
    consts_d = dt("consts", [128, 8], F32, kind="ExternalInput").ap()

    xn_scr = dt("xn_scr", [T, C], BF16, kind="Internal").ap()
    probs_scr = dt("probs_scr", [T, NE], F32, kind="Internal").ap()
    idx_scr = dt("idx_scr", [NE * CAP, 1], I32, kind="Internal").ap()
    g_scr = dt("g_scr", [T, C], F32, kind="Internal").ap()
    w1bf = dt("w1bf", [NE, C, HF], BF16, kind="Internal").ap()
    w2bf = dt("w2bf", [NE, HF, C], BF16, kind="Internal").ap()
    s_scr = dt("s_scr", [T, C], F32, kind="Internal").ap()
    y_d = dt("y", [T, C], F32, kind="ExternalOutput").ap()

    dbg = {}

    def dbgout(name, shape, dty):
        dbg[name] = dt("dbg_" + name, shape, dty, kind="ExternalOutput").ap()
        return dbg[name]

    gctx = ExitStack()

    def sb(name, shape, dty, ctx=None):
        pre = nc.sbuf_base
        h = (ctx or gctx).enter_context(nc.sbuf_tensor(name, shape, dty))
        ADDRS[name] = (pre, nc.sbuf_base)
        return h

    def ps(name, shape, dty, ctx=None):
        return (ctx or gctx).enter_context(nc.psum_tensor(name, shape, dty))

    sem = Env(nc, gctx, future)

    ident = sb("ident_s", [128, 128], BF16)
    onesf = sb("onesf_s", [1, 128], F32)
    onescf = sb("onescf_s", [128, 1], F32)
    consts = sb("consts_s", [128, 8], F32)
    nc.const_aps.register(F32, -1.0, consts[:, 0:1])
    nc.const_aps.register(F32, 1e-5, consts[:, 1:2])
    nc.const_aps.register(F32, 1e-12, consts[:, 2:3])
    nc.const_aps.register(F32, 0.125, consts[:, 3:4])
    nc.const_aps.register(F32, -0.5, consts[:, 4:5])
    nc.const_aps.register(F32, -1.0, consts[:, 5:6]) if False else None
    bias_bc = sb("bias_bc", [128, C], F32)
    BCX = [None]

    def BC(j):
        if j == 0:
            return bias_bc[:, :]
        return BCX[0][:, (j - 1) * C: j * C]

    stats = sb("stats", [128, 32], F32)
    gwsb = sb("gwsb", [128, 8 * NE], F32)
    idx_sb = sb("idx_sb", [128, NE * 4], I32)
    identf = sb("identf", [128, 128], F32)
    identh = sb("identh", [16, 16], F16)
    lo = sb("lo", [NE, 1], F32)
    hi = sb("hi", [NE, 1], F32)
    mid = sb("mid", [NE, 1], F32)
    cnt = sb("cnt", [NE, 1], F32)
    bm = sb("bm", [NE, 1], F32)
    msku8 = sb("msku8", [NE, 1], mybir.dt.uint8)
    c1 = sb("c1", [NE, 1], F32)
    takeb = sb("takeb", [NE, 1], F32)

    B = [None]

    def on(eng_name, f):
        getattr(B[0], eng_name)(f)

    def fence(tag):
        for eng_name in ENGS:
            def _(e):
                for smn in list(sem.h.keys()):
                    if sem.c[smn] > 0:
                        e.wait_ge(sem.h[smn], sem.c[smn])
            on(eng_name, _)

    pKV = ExitStack()
    knT = sb("knT", [128, 8 * S], BF16, pKV)
    v_tm = sb("v_tm", [128, 16 * 1040], BF16, pKV)

    with nc.Block() as block:
        B[0] = block

        p0 = ExitStack()
        srcT_bf = sb("srcT_bf", [128, 8 * S], BF16, p0)
        valT_bf = sb("valT_bf", [128, 8 * S], BF16, p0)
        Wk_bf = sb("Wk_bf", [128, 8 * C], BF16, p0)
        Wv_bf = sb("Wv_bf", [128, 8 * C], BF16, p0)
        eA = sb("eA", [128, C], F32, p0)
        eB = sb("eB", [128, C], F32, p0)
        eC = sb("eC", [128, C], F32, p0)
        eD = sb("eD", [128, 16], F32, p0)
        eE = sb("eE", [128, 16], F32, p0)
        eF = sb("eF", [128, 16], F32, p0)
        kn_bf = sb("kn_bf", [128, C], BF16, p0)
        bkv_bc = sb("bkv_bc", [128, 2 * C], F32, p0)
        ps_kv = ps("ps_kv", [128, C], F32, p0)
        ps_kv2 = ps("ps_kv2", [128, C], F32, p0)
        ps_tr0 = ps("ps_tr0", [128, C], BF16, p0)

        def _(e):
            sem.dma(e, "qS", ident[:], ident_d[:, :])
            sem.dma(e, "qS", onesf[:], onesf_d[:, :])
            sem.dma(e, "qS", onescf[:], onescf_d[:, :])
            sem.dma(e, "qS", consts[:], consts_d[:, :])
            sem.dma(e, "qS", BC(0), biases_d[0:1, 0:C].broadcast_to([128, C]))
            for jj, j in enumerate((1, 2)):
                sem.dma(e, "qS", bkv_bc[:, jj * C:(jj + 1) * C],
                        biases_d[0:1, j * C:(j + 1) * C].broadcast_to([128, C]))
            sem.dma(e, "qS", gwsb[:].rearrange("p (kc e) -> p kc e", kc=8),
                    gate_d.rearrange("(kc p) e -> p kc e", p=128),
                    mname="ld_consts")
        on("sync", _)

        def _(e):
            sem.dma(e, "qG", srcT_bf[:].rearrange("p (kc s) -> p kc s", kc=8),
                    srcT_d.rearrange("(kc p) s -> p kc s", p=128))
            sem.dma(e, "qG", valT_bf[:].rearrange("p (kc s) -> p kc s", kc=8),
                    valT_d.rearrange("(kc p) s -> p kc s", p=128))
            sem.dma(e, "qG", Wk_bf[:].rearrange("p (kc a) -> p kc a", kc=8),
                    Wk_d.rearrange("(kc p) a -> p kc a", p=128))
            sem.dma(e, "qG", Wv_bf[:].rearrange("p (kc a) -> p kc a", kc=8),
                    Wv_d.rearrange("(kc p) a -> p kc a", p=128),
                    mname="ld_kv")
        on("gpsimd", _)

        def _(e):
            sem.wait(e, "ld_consts")
            e.tensor_copy(identf[:, :], ident[:, :])
            inst = e.tensor_copy(identh[:, :], ident[0:16, 0:16])
            sem.incm(inst, "vS", "identf")
        on("vector", _)

        def _(e):
            inst = e.memset(v_tm[:, :], 1.0)
            sem.incm(inst, "vS", "v_ones")
        on("vector", _)

        for sc in range(16):
            def _(e, sc=sc):
                if sc == 0:
                    sem.wait(e, "ld_kv")
                else:
                    sem.wait(e, "kadd%d" % (sc - 1))
                for kc in range(8):
                    for nh in range(2):
                        inst = e.matmul(ps_kv[:, nh * 512:(nh + 1) * 512],
                                        srcT_bf[:, kc * S + sc * 128: kc * S + sc * 128 + 128],
                                        Wk_bf[:, kc * C + nh * 512: kc * C + (nh + 1) * 512],
                                        start=(kc == 0), stop=(kc == 7))
                sem.incm(inst, "tS", "kmm%d" % sc)
                if sc >= 1:
                    sem.wait(e, "vadd%d" % (sc - 1))
                for kc in range(8):
                    for nh in range(2):
                        inst = e.matmul(ps_kv2[:, nh * 512:(nh + 1) * 512],
                                        valT_bf[:, kc * S + sc * 128: kc * S + sc * 128 + 128],
                                        Wv_bf[:, kc * C + nh * 512: kc * C + (nh + 1) * 512],
                                        start=(kc == 0), stop=(kc == 7))
                sem.incm(inst, "tS", "vmm%d" % sc)
            on("tensor", _)

            def _(e, sc=sc):
                sem.wait(e, "kmm%d" % sc)
                if sc == 0:
                    sem.wait(e, "ld_consts")
                inst = e.tensor_add(eA[:, :], ps_kv[:, :], bkv_bc[:, 0:C])
                sem.incm(inst, "vS", "kadd%d" % sc)
                e.wait_ge(sem.h["vS"], sem.c["vS"])
                inst = e.tensor_scalar_min(eB[:, :], eA[:, :], 0.0)
                sem.incm(inst, "vS", "kmin%d" % sc)
            on("vector", _)

            def _(e, sc=sc):
                sem.wait(e, "kmin%d" % sc)
                inst = e.activation(eC[:, :], eB[:, :], AF.Exp)
                sem.incm(inst, "sS", "kexp%d" % sc)
            on("scalar", _)

            def _(e, sc=sc):
                sem.wait(e, "kexp%d" % sc)
                sem.vself(e, e.scalar_tensor_tensor(eB[:, :], eA[:, :], 0.0, eC[:, :],
                                                    ALU.max, ALU.add))
                sem.vself(e, e.tensor_scalar_add(eB[:, :], eB[:, :], -1.0))
                sem.vself(e, e.tensor_mul(eC[:, :], eB[:, :], eB[:, :]))
                inst = e.tensor_reduce(eD[:, :], eC[:, :].rearrange("p (a b) -> p a b", b=64),
                                       AX.X, ALU.add)
                sem.incm(inst, "vS", "kss%d" % sc)
            on("vector", _)

            def _(e, sc=sc):
                sem.wait(e, "kss%d" % sc)
                inst = e.activation(eE[:, :], eD[:, :], AF.Ln)
                sem.incm(inst, "sS", "kln%d" % sc)
                inst = e.activation(eF[:, :], eE[:, :], AF.Exp, scale=-0.5)
                sem.incm(inst, "sS", "ksrt%d" % sc)
            on("scalar", _)

            def _(e, sc=sc):
                sem.wait(e, "ksrt%d" % sc)
                if sc >= 1:
                    sem.wait(e, "ktr%d" % (sc - 1))
                for h in range(16):
                    inst = e.tensor_scalar(kn_bf[:, h * 64:(h + 1) * 64],
                                           eB[:, h * 64:(h + 1) * 64],
                                           eF[:, h:h + 1], None, ALU.mult)
                sem.incm(inst, "vS", "kn%d" % sc)
            on("vector", _)

            def _(e, sc=sc):
                sem.wait(e, "kn%d" % sc)
                if sc == 0:
                    sem.wait(e, "ld_consts")
                if sc >= 1:
                    sem.wait(e, "ktrc%d" % (sc - 1))
                for fc in range(8):
                    inst = e.transpose(ps_tr0[:, fc * 128:(fc + 1) * 128],
                                       kn_bf[:, fc * 128:(fc + 1) * 128], ident[:, :])
                sem.incm(inst, "tS", "ktr%d" % sc)
            on("tensor", _)

            def _(e, sc=sc):
                sem.wait(e, "ktr%d" % sc)
                inst = e.tensor_copy(
                    knT[:, :].rearrange("p (fc s) -> p fc s", fc=8)[:, :, sc * 128:(sc + 1) * 128],
                    ps_tr0[:, :].rearrange("p (fc t) -> p fc t", t=128))
                sem.incm(inst, "vS", "ktrc%d" % sc)
                sem.wait(e, "vmm%d" % sc)
                if sc == 0:
                    sem.wait(e, "v_ones")
                inst = e.tensor_add(eA[:, :], ps_kv2[:, :], bkv_bc[:, C:2 * C])
                sem.incm(inst, "vS", "vadd%d" % sc)
                inst = e.tensor_copy(
                    v_tm[:, :].rearrange("p (sc h x) -> p sc h x", sc=16, x=65)[:, sc, :, 0:64],
                    eA[:, :].rearrange("p (h x) -> p h x", x=64))
                sem.incm(inst, "vS", "vcp%d" % sc)
            on("vector", _)

        if DEBUG >= 1 and PHASES >= 1:
            d_knT = dbgout("knT", [128, 8 * S], F32)
            d_v = dbgout("v", [128, 16 * 1040], F32)

            def _(e):
                sem.wait(e, "ktrc15")
                sem.wait(e, "vcp15")
                sem.dma(e, "qD", d_knT[:, :], knT[:, :])
                sem.dma(e, "qD", d_v[:, :], v_tm[:, :])
            on("gpsimd", _)

        fence("p0_end")
        p0.close()

        pQ = ExitStack()
        qnT = sb("qnT", [128, 16 * C], BF16, pQ)
        pA = ExitStack()
        xnT = sb("xnT", [128, 8 * T], BF16, pA)

        if PHASES >= 2:
            pR = ExitStack()
            probsT = sb("probsT", [NE, T], F32, pR)
            cmpw = sb("cmpw", [NE, T], F16, pR)
            p1 = ExitStack()
            tmp1 = sb("tmp1", [128, 1], F32, p1)
            tmp2 = sb("tmp2", [128, 1], F32, p1)
            tmp3 = sb("tmp3", [128, 1], F32, p1)
            sd1 = sb("sd1", [128, 1], F32, p1)
            negcs = sb("negcs", [NE, 1], F32, p1)
            p1m3 = ExitStack()
            mu_bc = sb("mu_bc", [128, T], F32, p1m3)
            rstd_bc = sb("rstd_bc", [128, T], F32, p1m3)
            p1m1 = ExitStack()
            xtile = [sb("xtile%d" % i, [128, C], F32, p1m1) for i in range(2)]
            x2t = sb("x2t", [128, C], F32, p1m1)
            xn_t = sb("xn_t", [128, C], BF16, p1m1)

            for i in range(16):
                def _(e, i=i):
                    if i >= 2:
                        sem.wait(e, "xcons%d" % (i - 2))
                    sem.dma(e, "qX%d" % (i % 2), xtile[i % 2][:, :],
                            x_tm[i * 128:(i + 1) * 128, :], mname="xld%d" % i)
                on("sync", _)

                def _(e, i=i):
                    sem.wait(e, "xld%d" % i)
                    if i >= 1:
                        sem.wait(e, "xvar%d" % (i - 1))
                    inst = e.activation(x2t[:, :], xtile[i % 2][:, :], AF.Square)
                    sem.incm(inst, "sS", "xsq%d" % i)
                on("scalar", _)

                def _(e, i=i):
                    sem.wait(e, "xld%d" % i)
                    sem.vself(e, e.tensor_reduce(tmp1[:, :], xtile[i % 2][:, :],
                                                 AX.X, ALU.add))
                    sem.wait(e, "xsq%d" % i)
                    sem.vself(e, e.tensor_reduce(tmp2[:, :], x2t[:, :], AX.X, ALU.add))
                    sem.vself(e, e.tensor_scalar_mul(stats[:, i:i + 1], tmp1[:, :],
                                                     1.0 / C))
                    sem.vself(e, e.tensor_scalar_mul(tmp2[:, :], tmp2[:, :], 1.0 / C))
                    sem.vself(e, e.tensor_mul(tmp3[:, :], stats[:, i:i + 1],
                                              stats[:, i:i + 1]))
                    inst = e.tensor_sub(tmp2[:, :], tmp2[:, :], tmp3[:, :])
                    sem.incm(inst, "vS", "xvar%d" % i)
                on("vector", _)

                def _(e, i=i):
                    sem.wait(e, "xvar%d" % i)
                    inst = e.activation(sd1[:, :], tmp2[:, :], AF.Sqrt, bias=1e-5)
                    sem.incm(inst, "sS", "xsd%d" % i)
                on("scalar", _)

                def _(e, i=i):
                    sem.wait(e, "xsd%d" % i)
                    sem.vself(e, e.reciprocal(stats[:, 16 + i:17 + i], sd1[:, :]))
                    if i >= 1:
                        sem.wait(e, "xnst%d" % (i - 1))
                    inst = e.tensor_scalar(xn_t[:, :], xtile[i % 2][:, :],
                                           stats[:, i:i + 1], stats[:, 16 + i:17 + i],
                                           ALU.subtract, ALU.mult)
                    sem.incm(inst, "vS", "xcons%d" % i)
                on("vector", _)

                def _(e, i=i):
                    sem.wait(e, "xcons%d" % i)
                    sem.dma(e, "qXn", xn_scr[i * 128:(i + 1) * 128, :], xn_t[:, :],
                            mname="xnst%d" % i)
                on("gpsimd", _)

            p1s1 = ExitStack()
            ps_st = ps("ps_st", [16, 128], BF16, p1s1)
            ps_st2 = ps("ps_st2", [16, 128], BF16, p1s1)
            p1m1.close()
            p1m2 = ExitStack()
            stats_bf = sb("stats_bf", [128, 32], BF16, p1m2)
            statsT = sb("statsT", [16, 128], BF16, p1m2)
            statsT2 = sb("statsT2", [16, 128], BF16, p1m2)
            sel16s = sb("sel16_s", [16, 16 * 128], BF16, p1m2)
            ps_mu = ps("ps_mu", [128, 512], F32, p1s1)
            ps_mu2 = ps("ps_mu2", [128, 512], F32, p1s1)

            def _(e):
                sem.wait(e, "xcons15")
                sem.wait(e, "xnst15")
                sem.dma(e, "qS", sel16s[:], sel16_d[:, :], mname="ld_sel16")
            on("sync", _)

            def _(e):
                sem.wait(e, "xcons15")
                inst = e.tensor_copy(stats_bf[:, :], stats[:, :])
                sem.incm(inst, "vS", "stats_bf")
            on("vector", _)

            def _(e):
                sem.wait(e, "stats_bf")
                e.transpose(ps_st[:, :], stats_bf[:, 0:16], ident[:, :])
                inst = e.transpose(ps_st2[:, :], stats_bf[:, 16:32], ident[:, :])
                sem.incm(inst, "tS", "sttr")
            on("tensor", _)

            def _(e):
                sem.wait(e, "sttr")
                e.tensor_copy(statsT[:, :], ps_st[:, :])
                inst = e.tensor_copy(statsT2[:, :], ps_st2[:, :])
                sem.incm(inst, "vS", "statsT")
            on("vector", _)

            for grp in range(4):
                def _(e, grp=grp):
                    sem.wait(e, "statsT")
                    sem.wait(e, "ld_sel16")
                    if grp >= 1:
                        sem.wait(e, "mubc%d" % (grp - 1))
                    for q in range(4):
                        i = grp * 4 + q
                        e.matmul(ps_mu[:, q * 128:(q + 1) * 128],
                                 sel16s[:, i * 128:(i + 1) * 128], statsT[:, :],
                                 start=True, stop=True)
                        inst = e.matmul(ps_mu2[:, q * 128:(q + 1) * 128],
                                        sel16s[:, i * 128:(i + 1) * 128], statsT2[:, :],
                                        start=True, stop=True)
                    sem.incm(inst, "tS", "mumm%d" % grp)
                on("tensor", _)

                def _(e, grp=grp):
                    sem.wait(e, "mumm%d" % grp)
                    e.tensor_copy(mu_bc[:, grp * 512:(grp + 1) * 512], ps_mu[:, :])
                    inst = e.tensor_copy(rstd_bc[:, grp * 512:(grp + 1) * 512], ps_mu2[:, :])
                    sem.incm(inst, "vS", "mubc%d" % grp)
                on("vector", _)

            p1s1.close()
            p1m2.close()
            p1m4 = ExitStack()
            xTt = [sb("xTt%d" % i, [128, T // 2], F32, p1m4) for i in range(2)]
            t1w = sb("t1w", [128, T // 2], F32, p1m4)
            p1s2 = ExitStack()
            ps_g = ps("ps_g", [NE, T], F32, p1s2)

            for it2 in range(16):
                kc, hf = it2 // 2, it2 % 2
                def _(e, it2=it2, kc=kc, hf=hf):
                    if it2 == 0:
                        sem.wait(e, "xcons15")
                        sem.wait(e, "xnst15")
                        sem.wait(e, "mumm3")
                    if it2 >= 2:
                        sem.wait(e, "xnTc%d" % (it2 - 2))
                        sem.wait(e, "gmm%d" % (it2 - 2))
                    sem.dma(e, "qT%d" % (it2 % 2), xTt[it2 % 2][:, :],
                            xT_d[kc * 128:(kc + 1) * 128, hf * 1024:(hf + 1) * 1024],
                            mname="xTld%d" % it2)
                on("sync", _)

                def _(e, it2=it2, kc=kc, hf=hf):
                    sem.wait(e, "xTld%d" % it2)
                    if it2 == 0:
                        sem.wait(e, "mubc3")
                    for q in range(2):
                        inst = e.matmul(ps_g[:, (hf * 2 + q) * 512:(hf * 2 + q + 1) * 512],
                                        gwsb[:, kc * NE:(kc + 1) * NE],
                                        xTt[it2 % 2][:, q * 512:(q + 1) * 512],
                                        start=(kc == 0), stop=(kc == 7))
                    sem.incm(inst, "tS", "gmm%d" % it2)
                on("tensor", _)

                def _(e, it2=it2, kc=kc, hf=hf):
                    sem.wait(e, "xTld%d" % it2)
                    sem.wait(e, "mubc3")
                    e.tensor_sub(t1w[:, :], xTt[it2 % 2][:, :],
                                 mu_bc[:, hf * 1024:(hf + 1) * 1024])
                    inst = e.tensor_mul(xnT[:, kc * T + hf * 1024: kc * T + (hf + 1) * 1024],
                                        t1w[:, :], rstd_bc[:, hf * 1024:(hf + 1) * 1024])
                    sem.incm(inst, "vS", "xnTc%d" % it2)
                on("vector", _)

            ps_cs = ps("ps_cs", [1, NE], F32, p1s2)
            ps_cst = ps("ps_cst", [NE, 128], F32, p1s2)
            p1m5 = ExitStack()
            gex = sb("gex", [NE, T], F32, p1m5)
            grecip = sb("grecip", [1, T], F16, p1m5)
            obf8 = sb("obf8", [1, NE], BF16, p1m5)

            def _(e):
                inst = e.memset(obf8[:, :], 1.0)
                sem.incm(inst, "vS", "obf8")
            on("vector", _)

            def _(e):
                sem.wait(e, "ld_consts")
                for kc in range(8):
                    inst = e.matmul(ps_cst[:, 0:1], gwsb[:, kc * NE:(kc + 1) * NE],
                                    onescf[:, 0:1],
                                    start=(kc == 0), stop=(kc == 7))
                sem.incm(inst, "tS", "cstr")
            on("tensor", _)

            def _(e):
                sem.wait(e, "cstr")
                inst = e.tensor_scalar_mul(negcs[:, :], ps_cst[:, 0:1], -1.0)
                sem.incm(inst, "vS", "negcs")
                e.wait_ge(sem.h["vS"], sem.c["vS"])
                sem.wait(e, "gmm15")
                sem.vself(e, e.scalar_tensor_tensor(gex[:, :], mu_bc[0:NE, :],
                                                    negcs[:, 0:1], ps_g[:, :],
                                                    ALU.mult, ALU.add))
                inst = e.tensor_mul(gex[:, :], gex[:, :], rstd_bc[0:NE, :])
                sem.incm(inst, "vS", "gates")
            on("vector", _)

            def _(e):
                sem.wait(e, "gates")
                inst = e.activation(gex[:, :], gex[:, :], AF.Exp)
                sem.incm(inst, "sS", "gexp")
            on("scalar", _)
            p1s2.close()
            p1s3 = ExitStack()
            ps_sum = ps("ps_sum", [1, T], F32, p1s3)
            ps_rb8 = ps("ps_rb8", [NE, T], F32, p1s3)

            def _(e):
                sem.wait(e, "gexp")
                for q in range(4):
                    inst = e.matmul(ps_sum[:, q * 512:(q + 1) * 512],
                                    onescf[0:NE, 0:1],
                                    gex[:, q * 512:(q + 1) * 512],
                                    start=True, stop=True)
                sem.incm(inst, "tS", "gsum")
            on("tensor", _)

            def _(e):
                sem.wait(e, "gsum")
                inst = e.activation(grecip[:, :], ps_sum[:, :], AF.Ln)
                sem.incm(inst, "sS", "gln")
                inst = e.activation(grecip[:, :], grecip[:, :], AF.Exp, scale=-1.0)
                sem.incm(inst, "sS", "grecip")
            on("scalar", _)

            def _(e):
                sem.wait(e, "grecip")
                sem.wait(e, "obf8")
                for q in range(4):
                    inst = e.matmul(ps_rb8[:, q * 512:(q + 1) * 512],
                                    obf8[0:1, :], grecip[:, q * 512:(q + 1) * 512],
                                    start=True, stop=True)
                sem.incm(inst, "tS", "grb")
            on("tensor", _)

            def _(e):
                sem.wait(e, "grb")
                inst = e.tensor_mul(probsT[:, :], gex[:, :], ps_rb8[:, :])
                sem.incm(inst, "vS", "probsT")
            on("vector", _)

            p1s3.close()
            ps_ptm = ps("ps_ptm", [128, 128], F32, p1)
            ptm = sb("ptm", [128, 128], F32, p1m5)

            def _(e):
                sem.wait(e, "probsT")
                for cch in range(16):
                    inst = e.transpose(ps_ptm[:, cch * 8:(cch + 1) * 8],
                                       probsT[:, cch * 128:(cch + 1) * 128], identf[0:NE, 0:NE])
                sem.incm(inst, "tS", "ptm_tr")
            on("tensor", _)

            def _(e):
                sem.wait(e, "ptm_tr")
                inst = e.tensor_copy(ptm[:, :], ps_ptm[:, :])
                sem.incm(inst, "vS", "ptm")
            on("vector", _)

            def _(e):
                sem.wait(e, "ptm")
                sem.dma(e, "qPr",
                        probs_scr[:, :].rearrange("(cch p) e -> p cch e", p=128),
                        ptm[:, :].rearrange("p (cch e) -> p cch e", e=NE),
                        mname="probs_st")
            on("gpsimd", _)

            if DEBUG >= 1:
                d_xnT = dbgout("xnT", [128, 8 * T], F32)
                d_stats = dbgout("stats", [128, 32], F32)
                d_statsT = dbgout("statsT", [16, 2 * 128], F32)
                d_mubc = dbgout("mubc", [128, 2 * T], F32)
                d_gex = dbgout("gex", [NE, T], F32)

                def _(e):
                    sem.wait(e, "xnTc15")
                    sem.dma(e, "qD", d_xnT[:, :], xnT[:, :])
                on("gpsimd", _)

                def _(e):
                    sem.wait(e, "statsT")
                    sem.dma(e, "qD", d_statsT[:, 0:128], statsT[:, :])
                    sem.dma(e, "qD", d_statsT[:, 128:256], statsT2[:, :])
                on("gpsimd", _)

                def _(e):
                    sem.wait(e, "xcons15")
                    sem.dma(e, "qD", d_stats[:, :], stats[:, :])
                    sem.wait(e, "mubc3")
                    sem.dma(e, "qD", d_mubc[:, 0:T], mu_bc[:, :])
                    sem.dma(e, "qD", d_mubc[:, T:2 * T], rstd_bc[:, :])
                    sem.wait(e, "gates")
                    sem.dma(e, "qD", d_gex[:, :], gex[:, :])
                on("sync", _)
                d_probsT = dbgout("probsT", [NE, T], F32)
                d_lohi = dbgout("lohi", [NE, 2], F32)
                d_bis = dbgout("bis", [NE, 8], F32)
                d_idx = dbgout("idx", [NE * CAP, 1], I32)
                d_mask = dbgout("mask", [NE, T], F16)

                def _(e):
                    sem.wait(e, "xnTc7")
                    sem.wait(e, "routing_rows")
                    sem.dma(e, "qD", d_probsT[:, :], probsT[:, :])
                    sem.dma(e, "qD", d_bis[:, :], bisdbg[:, :])
                    sem.dma(e, "qD", d_lohi[:, 0:1], lo[:, :],
                            allow_slow_non_contiguous=True)
                    sem.dma(e, "qD", d_lohi[:, 1:2], hi[:, :],
                            allow_slow_non_contiguous=True)
                    sem.dma(e, "qD", d_mask[:, :], cmpw[:, :])
                    sem.wait(e, "idx_sb")
                    sem.dma(e, "qD", d_idx[:, :], idx_scr[:, :])
                on("sync", _)

            p1m5.close()
            p1m4.close()
            p1m3.close()
            fence("p1_end")
            p1.close()

        if PHASES >= 3:
            p2aw = ExitStack()
            Wq_bf = sb("Wq_bf", [128, 8 * C], BF16, p2aw)
            qA = sb("qA", [128, C], F32, p2aw)
            qB = sb("qB", [128, C], F32, p2aw)
            qC = sb("qC", [128, C], F32, p2aw)
            qD = sb("qD", [128, 16], F32, p2aw)
            qE = sb("qE", [128, 16], F32, p2aw)
            qF = sb("qF", [128, 16], F32, p2aw)
            qn_bf = sb("qn_bf", [128, C], BF16, p2aw)
            ps_q = ps("ps_q", [128, C], F32, p2aw)
            ps_trq = ps("ps_trq", [128, C], BF16, p2aw)

            def _(e):
                sem.dma(e, "qQO", Wq_bf[:].rearrange("p (kc a) -> p kc a", kc=8),
                        Wq_d.rearrange("(kc p) a -> p kc a", p=128), mname="ld_q")
            on("gpsimd", _)

            for u in range(16):
                def _(e, u=u):
                    if u == 0:
                        sem.wait(e, "ld_q")
                    else:
                        sem.wait(e, "qadd%d" % (u - 1))
                    for kc in range(8):
                        for nh in range(2):
                            inst = e.matmul(
                                ps_q[:, nh * 512:(nh + 1) * 512],
                                xnT[:, kc * T + u * 128: kc * T + u * 128 + 128],
                                Wq_bf[:, kc * C + nh * 512: kc * C + (nh + 1) * 512],
                                start=(kc == 0), stop=(kc == 7))
                    sem.incm(inst, "tS", "qmm%d" % u)
                    if u >= 1:
                        sem.wait(e, "qn%d" % (u - 1))
                        if u >= 2:
                            sem.wait(e, "qtrc%d" % (u - 2))
                        for fc in range(8):
                            inst = e.transpose(ps_trq[:, fc * 128:(fc + 1) * 128],
                                               qn_bf[:, fc * 128:(fc + 1) * 128],
                                               ident[:, :])
                        sem.incm(inst, "tS", "qtr%d" % (u - 1))
                on("tensor", _)

                def _(e, u=u):
                    sem.wait(e, "qmm%d" % u)
                    inst = e.tensor_add(qA[:, :], ps_q[:, :], BC(0))
                    sem.incm(inst, "vS", "qadd%d" % u)
                    e.wait_ge(sem.h["vS"], sem.c["vS"])
                    inst = e.tensor_scalar_min(qB[:, :], qA[:, :], 0.0)
                    sem.incm(inst, "vS", "qmin%d" % u)
                    if u >= 1:
                        sem.wait(e, "qtr%d" % (u - 1))
                        inst = e.tensor_copy(
                            qnT[:, :].rearrange("p (fc t) -> p fc t", fc=8)[:, :, (u - 1) * 128:u * 128],
                            ps_trq[:, :].rearrange("p (fc t) -> p fc t", t=128))
                        sem.incm(inst, "vS", "qtrc%d" % (u - 1))
                on("vector", _)

                def _(e, u=u):
                    sem.wait(e, "qmin%d" % u)
                    inst = e.activation(qC[:, :], qB[:, :], AF.Exp)
                    sem.incm(inst, "sS", "qexp%d" % u)
                on("scalar", _)

                def _(e, u=u):
                    sem.wait(e, "qexp%d" % u)
                    sem.vself(e, e.scalar_tensor_tensor(qB[:, :], qA[:, :], 0.0,
                                                        qC[:, :], ALU.max, ALU.add))
                    sem.vself(e, e.tensor_scalar_add(qB[:, :], qB[:, :], -1.0))
                    sem.vself(e, e.tensor_mul(qC[:, :], qB[:, :], qB[:, :]))
                    inst = e.tensor_reduce(qD[:, :],
                                           qC[:, :].rearrange("p (a b) -> p a b", b=64),
                                           AX.X, ALU.add)
                    sem.incm(inst, "vS", "qss%d" % u)
                on("vector", _)

                def _(e, u=u):
                    sem.wait(e, "qss%d" % u)
                    inst = e.activation(qE[:, :], qD[:, :], AF.Ln)
                    sem.incm(inst, "sS", "qln%d" % u)
                    inst = e.activation(qF[:, :], qE[:, :], AF.Exp, scale=-0.5)
                    sem.incm(inst, "sS", "qsrt%d" % u)
                on("scalar", _)

                def _(e, u=u):
                    sem.wait(e, "qsrt%d" % u)
                    if u >= 1:
                        sem.wait(e, "qtr%d" % (u - 1))
                    for h in range(16):
                        inst = e.tensor_scalar(qn_bf[:, h * 64:(h + 1) * 64],
                                               qB[:, h * 64:(h + 1) * 64],
                                               qF[:, h:h + 1], None, ALU.mult)
                    sem.incm(inst, "vS", "qn%d" % u)
                on("vector", _)

            def _(e):
                sem.wait(e, "qn15")
                sem.wait(e, "qtrc14")
                for fc in range(8):
                    inst = e.transpose(ps_trq[:, fc * 128:(fc + 1) * 128],
                                       qn_bf[:, fc * 128:(fc + 1) * 128], ident[:, :])
                sem.incm(inst, "tS", "qtr15")
            on("tensor", _)

            def _(e):
                sem.wait(e, "qtr15")
                inst = e.tensor_copy(
                    qnT[:, :].rearrange("p (fc t) -> p fc t", fc=8)[:, :, 15 * 128:16 * 128],
                    ps_trq[:, :].rearrange("p (fc t) -> p fc t", t=128))
                sem.incm(inst, "vS", "qtrc15")
            on("vector", _)

            pR2 = ExitStack()
            mlo = sb("mlo", [NE, T], F16, pR2)
            scanw = sb("scanw", [NE, T], F16, pR2)
            srcrow = sb("srcrow", [NE, T], F16, pR2)
            dstrow = sb("dstrow", [NE, T], F16, pR2)
            band = dstrow
            iota8s = sb("iota8_s", [NE, T], F16, pR2)
            if DEBUG >= 1:
                bisdbg = sb("bisdbg", [NE, 8], F32, pR2)

            def _(e):
                sem.wait(e, "probsT")
                sem.wait(e, "probs_st")
                sem.dma(e, "qS", iota8s[:], iota8_d[:, :], mname="ld_iota")
            on("sync", _)

            def emit_bis(e, kk):
                V = lambda inst: sem.vself(e, inst)
                if kk == 0:
                    V(e.memset(lo[:, :], 0.0))
                    V(e.memset(hi[:, :], 1.0))
                    sem.wait(e, "probsT")
                    sem.wait(e, "probs_st")
                    iters = range(0, 2)
                elif kk <= 12:
                    iters = range(2 * kk, 2 * kk + 2)
                else:
                    iters = range(0, 0)
                for it in iters:
                    V(e.tensor_add(mid[:, :], lo[:, :], hi[:, :]))
                    V(e.tensor_scalar_mul(mid[:, :], mid[:, :], 0.5))
                    V(e.tensor_scalar(cmpw[:, :], probsT[:, :], mid[:, 0:1], 0.0,
                                      ALU.is_gt, ALU.add, accum_out=cnt[:, :]))
                    V(e.tensor_scalar(bm[:, :], cnt[:, :], 512.0, None, ALU.is_gt))
                    V(e.tensor_copy(msku8[:, :], bm[:, :]))
                    V(e.select(lo[:, :], msku8[:, :], mid[:, :], lo[:, :]))
                    V(e.select(hi[:, :], msku8[:, :], hi[:, :], mid[:, :]))
                if kk == 13:
                    sem.wait(e, "ld_iota")
                    V(e.tensor_scalar(cmpw[:, :], probsT[:, :], hi[:, 0:1], 0.0,
                                      ALU.is_gt, ALU.add, accum_out=c1[:, :]))
                    V(e.tensor_scalar(mlo[:, :], probsT[:, :], lo[:, 0:1], None, ALU.is_gt))
                    V(e.tensor_sub(band[:, :], mlo[:, :], cmpw[:, :]))
                    V(e.tensor_tensor_scan(scanw[:, :], band[:, :], band[:, :], 0.0,
                                           ALU.add, ALU.bypass))
                    V(e.tensor_scalar(takeb[:, :], c1[:, :], -1.0, 512.0, ALU.mult, ALU.add))
                    V(e.tensor_scalar(mlo[:, :], scanw[:, :], takeb[:, 0:1], None, ALU.is_le))
                    V(e.tensor_mul(band[:, :], band[:, :], mlo[:, :]))
                    V(e.tensor_add(cmpw[:, :], cmpw[:, :], band[:, :]))
                    V(e.tensor_tensor_scan(scanw[:, :], cmpw[:, :], cmpw[:, :], 0.0,
                                           ALU.add, ALU.bypass))
                    V(e.tensor_scalar(scanw[:, :], scanw[:, :], -1.0, None, ALU.add))
                    V(e.tensor_mul(dstrow[:, :], scanw[:, :], cmpw[:, :]))
                    V(e.tensor_scalar(mlo[:, :], cmpw[:, :], -BIG, BIG, ALU.mult, ALU.add))
                    V(e.tensor_add(dstrow[:, :], dstrow[:, :], mlo[:, :]))
                    V(e.tensor_mul(srcrow[:, :], iota8s[:, :], cmpw[:, :]))
                    inst = e.tensor_add(srcrow[:, :], srcrow[:, :], mlo[:, :])
                    sem.incm(inst, "vS", "routing_rows")

            def emit_sdtr(e):
                sem.wait(e, "routing_rows")
                for cch in range(16):
                    e.transpose(ps_sd[:, cch * 8:(cch + 1) * 8],
                                srcrow[:, cch * 128:(cch + 1) * 128], identh[0:NE, 0:NE])
                    inst = e.transpose(ps_sd[:, 128 + cch * 8:128 + (cch + 1) * 8],
                                       dstrow[:, cch * 128:(cch + 1) * 128], identh[0:NE, 0:NE])
                sem.incm(inst, "tS", "sd_tr")

            def emit_sdicopy(e):
                sem.wait(e, "sd_tr")
                e.tensor_copy(sdf[:, :], ps_sd[:, :])
                inst = e.tensor_copy(sdi[:, :], sdf[:, :])
                sem.incm(inst, "vS", "sdi")

            def emit_compaction(e):
                sem.wait(e, "sdi")
                breg = e.alloc_register("bnd")
                e.reg_mov(breg, CAP - 1)
                for ex in range(NE):
                    for cch in range(16):
                        inst = e.indirect_dma_start(
                            out=idx_scr[:, :],
                            out_offset=IndirectOffsetOnAxis(
                                sdi[:, 128 + cch * 8 + ex: 128 + cch * 8 + ex + 1], 0),
                            in_=sdi[:, cch * 8 + ex: cch * 8 + ex + 1],
                            in_offset=None,
                            element_offset=ex * CAP,
                            bounds_check=breg,
                            oob_is_err=False,
                        )
                        sem.inc(inst, "qG", 16)
                e.free_register(breg)
                sem.mark("idx_done", "qG")
                sem.waitq(e, "qG")
                inst = e.dma_start(
                    out=idx_sb[:, :].rearrange("p (ex j) -> p ex j", j=4),
                    in_=idx_scr[:, 0].rearrange("(ex p j) -> p ex j", ex=NE, p=128))
                sem.inc(inst, "qG", 16)
                sem.mark("idx_sb", "qG")

            pR2.close()
            p2aw.close()
            pR.close()
            pA.close()

            DO_P2B = PHASES >= 4
            p2b = ExitStack()
            Wo_bf = sb("Wo_bf", [128, 8 * C], BF16, p2b)
            expA = [sb("expA0", [128, 16 * 512], BF16, p2b)]
            bias_bc2 = sb("bias_bc2", [128, 3 * C], F32, p2b)
            BCX[0] = bias_bc2
            expA.append(sb("expA1", [128, 16 * 512], BF16, p2b))
            ctxT = sb("ctxT", [128, 8 * 512], BF16, p2b)
            rbv = sb("rbv", [1, 512], F32, p2b)
            rbc = sb("rbc", [64, 512], F32, p2b)
            rb_bf = sb("rb_bf", [1, 512], BF16, p2b)
            ones_bf1 = sb("ones_bf1", [1, 128], BF16, p2b)
            xres = sb("xres", [128, 4 * C], BF16, p2b)
            sA = sb("sA", [128, C], F32, p2b)
            sB = sb("sB", [128, C], F32, p2b)
            s_out = sb("s_out", [128, C], F32, p2b)
            st1 = sb("st1", [128, 1], F32, p2b)
            st2 = sb("st2", [128, 1], F32, p2b)
            st3 = sb("st3", [128, 1], F32, p2b)
            sdf = sb("sdf", [128, 256], F16, p2b)
            sdi = sb("sdi", [128, 256], I32, p2b)
            ps_s = ps("ps_s", [128, 4 * 512], F32, p2b)
            ps_c = [ps("ps_c%d" % i, [128, 512], F32, p2b) for i in range(2)]
            ps_rb = ps("ps_rb", [64, 512], F32, p2b)
            ps_sd = ps("ps_sd", [128, 256], F16, p2b)
            ps_o = ps_s

            def _(e):
                inst = e.memset(ones_bf1[:, :], 1.0)
                sem.incm(inst, "vS", "ones_bf1")
            on("vector", _)

            def _(e):
                for ex in range(NE):
                    sem.dma(e, "qWc", w1bf[ex, :, :], w1_d[ex, :, :])
                    sem.dma(e, "qWc", w2bf[ex, :, :], w2_d[ex, :, :])
                sem.mark("wcast", "qWc")
                sem.wait(e, "qmm15")
                sem.dma(e, "qQO", Wo_bf[:].rearrange("p (kc a) -> p kc a", kc=8),
                        Wo_d.rearrange("(kc p) a -> p kc a", p=128), mname="ld_o")
                sem.wait(e, "routing_rows")
                for jj, j in enumerate((3, 4, 5)):
                    sem.dma(e, "qS", BC(jj + 1),
                            biases_d[0:1, j * C:(j + 1) * C].broadcast_to([128, C]),
                            mname="ld_bc2" if jj == 2 else None)
            on("gpsimd", _)

            on("gpsimd", emit_compaction)

            for ti in range(4 if DO_P2B else 0):
                def _(e, ti=ti):
                    if ti >= 1:
                        sem.wait(e, "x2a%d_3" % (ti - 1))
                    else:
                        sem.wait(e, "routing_rows")
                    sem.dma(e, "qR", xres[:, :].rearrange("p (sub c) -> p sub c", c=C),
                            xn_scr[ti * 512:(ti + 1) * 512, :].rearrange(
                                "(sub p) c -> p sub c", p=128),
                            mname="xres%d" % ti)
                on("sync", _)

                def pk(ti, h):
                    if h >= 0:
                        return (ti, h)
                    return (ti - 1, 16 + h)

                for k in range(18):
                    def _(e, ti=ti, k=k):
                        if 2 <= k:
                            h2 = k - 2
                            sem.wait(e, "crc%d_%d" % (ti, h2))
                            if ti == 0 and h2 == 0:
                                sem.wait(e, "ones_bf1")
                            p = pk(ti, h2 - 1)
                            if p[0] >= 0:
                                sem.wait(e, "cevac%d_%d" % p)
                            inst = e.matmul(ps_rb[:, :], ones_bf1[0:1, 0:64],
                                            rb_bf[0:1, :], start=True, stop=True)
                            sem.incm(inst, "tS", "crb%d_%d" % (ti, h2))
                        if 1 <= k <= 16:
                            h1 = k - 1
                            p = pk(ti, h1 - 2)
                            if p[0] >= 0:
                                sem.wait(e, "cevac%d_%d" % p)
                            for sc in range(16):
                                if sc % 4 == 0:
                                    sem.wait(e, "sexp%d_%d_%d" % (ti, h1, min(sc // 2 + 1, 7)))
                                inst = e.matmul(
                                    ps_c[h1 % 2][0:65, :],
                                    v_tm[:, sc * 1040 + h1 * 65: sc * 1040 + (h1 + 1) * 65],
                                    expA[h1 % 2][:, sc * 512:(sc + 1) * 512],
                                    start=(sc == 0), stop=(sc == 15))
                            sem.incm(inst, "tS", "cmm%d_%d" % (ti, h1))
                        if k <= 15:
                            h = k
                            fc = h // 2
                            hp = (h % 2) * 64
                            if ti == 0 and h == 0:
                                sem.wait(e, "qtrc15")
                                sem.wait(e, "ktrc15")
                                sem.wait(e, "vcp15")
                            if h == 0 and ti >= 1:
                                sem.wait(e, "x2a%d_3" % (ti - 1))
                            for g in range(16):
                                if g == 0:
                                    p = pk(ti, h - 1)
                                    if p[0] >= 0:
                                        sem.wait(e, "sexp%d_%d_%d" % (p[0], p[1], 7))
                                elif g % 4 == 0:
                                    sem.wait(e, "sexp%d_%d_%d" % (ti, h, g // 2 - 1))
                                inst = e.matmul(
                                    ps_s[:, (g % 4) * 512:(g % 4 + 1) * 512],
                                    knT[hp:hp + 64,
                                        fc * S + g * 128: fc * S + (g + 1) * 128],
                                    qnT[hp:hp + 64,
                                        fc * T + ti * 512: fc * T + ti * 512 + 512],
                                    start=True, stop=True)
                                sem.incm(inst, "tS", "smm%d_%d_%d" % (ti, h, g))
                        if ti == 0 and k == 15:
                            emit_sdtr(e)
                    on("tensor", _)

                    def _(e, ti=ti, k=k):
                        if k <= 15:
                            h = k
                            p = pk(ti, h - 2)
                            if p[0] >= 0:
                                sem.wait(e, "cmm%d_%d" % p)
                            for gp in range(8):
                                sem.wait(e, "smm%d_%d_%d" % (ti, h, 2 * gp + 1))
                                inst = e.activation(
                                    expA[h % 2][:, gp * 1024:(gp + 1) * 1024],
                                    ps_s[:, (2 * gp % 4) * 512:(2 * gp % 4 + 2) * 512],
                                    AF.Exp, scale=0.125)
                                sem.incm(inst, "sS", "sexp%d_%d_%d" % (ti, h, gp))
                        if 1 <= k <= 16:
                            h1 = k - 1
                            sem.wait(e, "cmm%d_%d" % (ti, h1))
                            p = pk(ti, h1 - 1)
                            if p[0] >= 0:
                                sem.wait(e, "crb%d_%d" % p)
                            inst = e.activation(rbv[0:1, :], ps_c[h1 % 2][64:65, :], AF.Ln)
                            sem.incm(inst, "sS", "cln%d_%d" % (ti, h1))
                            inst = e.activation(rb_bf[0:1, :], rbv[0:1, :], AF.Exp,
                                                scale=-1.0)
                            sem.incm(inst, "sS", "crc%d_%d" % (ti, h1))
                    on("scalar", _)

                    def _(e, ti=ti, k=k):
                        if 2 <= k:
                            h2 = k - 2
                            fc = h2 // 2
                            hp = (h2 % 2) * 64
                            sem.wait(e, "crb%d_%d" % (ti, h2))
                            sem.vself(e, e.tensor_copy(rbc[:, :], ps_rb[:, :]))
                            inst = e.tensor_mul(ctxT[hp:hp + 64, fc * 512:(fc + 1) * 512],
                                                ps_c[h2 % 2][0:64, :], rbc[:, :])
                            sem.incm(inst, "vS", "cevac%d_%d" % (ti, h2))
                        if ti == 0 and k <= 13:
                            emit_bis(e, k)
                        if ti == 0 and k == 16:
                            emit_sdicopy(e)
                    on("vector", _)

                for sub in range(4):
                    def _(e, ti=ti, sub=sub):
                        if sub == 0:
                            sem.wait(e, "cevac%d_15" % ti)
                            sem.wait(e, "ld_o")
                            if ti >= 1:
                                sem.wait(e, "x2a%d_3" % (ti - 1))
                        else:
                            sem.wait(e, "x2a%d_%d" % (ti, sub - 1))
                        for fc in range(8):
                            for nh in range(2):
                                inst = e.matmul(
                                    ps_o[:, nh * 512:(nh + 1) * 512],
                                    ctxT[:, fc * 512 + sub * 128: fc * 512 + sub * 128 + 128],
                                    Wo_bf[:, fc * C + nh * 512: fc * C + (nh + 1) * 512],
                                    start=(fc == 0), stop=(fc == 7))
                        sem.incm(inst, "tS", "omm%d_%d" % (ti, sub))
                    on("tensor", _)

                    def _(e, ti=ti, sub=sub):
                        sem.wait(e, "omm%d_%d" % (ti, sub))
                        sem.wait(e, "xres%d" % ti)
                        if ti == 0 and sub == 0:
                            sem.wait(e, "ld_bc2")
                            sem.wait(e, "sd_tr")
                        sem.vself(e, e.tensor_add(sA[:, :], ps_o[:, 0:C], BC(1)))
                        inst = e.tensor_add(sA[:, :], sA[:, :],
                                            xres[:, sub * C:(sub + 1) * C])
                        sem.incm(inst, "vS", "x2a%d_%d" % (ti, sub))
                        e.wait_ge(sem.h["vS"], sem.c["vS"])
                        sem.vself(e, e.tensor_reduce(st1[:, :], sA[:, :], AX.X, ALU.add))
                        sem.vself(e, e.tensor_scalar_mul(st1[:, :], st1[:, :], 1.0 / C))
                        sem.vself(e, e.tensor_mul(sB[:, :], sA[:, :], sA[:, :]))
                        sem.vself(e, e.tensor_reduce(st2[:, :], sB[:, :], AX.X, ALU.add))
                        sem.vself(e, e.tensor_scalar_mul(st2[:, :], st2[:, :], 1.0 / C))
                        sem.vself(e, e.tensor_mul(st3[:, :], st1[:, :], st1[:, :]))
                        inst = e.tensor_sub(st2[:, :], st2[:, :], st3[:, :])
                        sem.incm(inst, "vS", "ovar%d_%d" % (ti, sub))
                    on("vector", _)

                    def _(e, ti=ti, sub=sub):
                        sem.wait(e, "ovar%d_%d" % (ti, sub))
                        inst = e.activation(st3[:, :], st2[:, :], AF.Ln, bias=1e-12)
                        sem.incm(inst, "sS", "oln%d_%d" % (ti, sub))
                        inst = e.activation(st2[:, :], st3[:, :], AF.Exp, scale=-0.5)
                        sem.incm(inst, "sS", "osd%d_%d" % (ti, sub))
                    on("scalar", _)

                    def _(e, ti=ti, sub=sub):
                        sem.wait(e, "osd%d_%d" % (ti, sub))
                        if not (ti == 0 and sub == 0):
                            sem.wait(e, "sst%d_%d" % ((ti, sub - 1) if sub else (ti - 1, 3)))
                        sem.vself(e, e.tensor_scalar(sB[:, :], sA[:, :], st1[:, 0:1],
                                                     st2[:, 0:1], ALU.subtract, ALU.mult))
                        sem.vself(e, e.tensor_mul(sB[:, :], sB[:, :], BC(2)))
                        inst = e.tensor_add(s_out[:, :], sB[:, :], BC(3))
                        sem.incm(inst, "vS", "sfin%d_%d" % (ti, sub))
                    on("vector", _)

                    def _(e, ti=ti, sub=sub):
                        sem.wait(e, "sfin%d_%d" % (ti, sub))
                        sem.dma(e, "qSs",
                                s_scr[ti * 512 + sub * 128: ti * 512 + (sub + 1) * 128, :],
                                s_out[:, :], mname="sst%d_%d" % (ti, sub))
                    on("sync", _)

            if DEBUG >= 1 and DO_P2B:
                d_s = dbgout("s", [T, C], F32)

                def _(e):
                    sem.wait(e, "sst3_3")
                    sem.waitq(e, "qSs")
                    for i in range(16):
                        sem.dma(e, "qD", d_s[i * 128:(i + 1) * 128, :],
                                s_scr[i * 128:(i + 1) * 128, :])
                on("sync", _)

            fence("p2_end")
            p2b.close()
        if PHASES < 3:
            pA.close()
        pQ.close()
        pKV.close()

        if PHASES >= 6:
            p3 = ExitStack()
            w1sb = [sb("w1sb%d" % i, [128, 8 * 1024], BF16, p3) for i in range(2)]
            w2sb = sb("w2sb", [128, 32 * 1024], BF16, p3)
            xin_tm = sb("xin_tm", [128, 4 * C], BF16, p3)
            xinT = sb("xinT", [128, 8 * 512], BF16, p3)
            p_sb = sb("p_sb", [128, 4 * NE], F32, p3)
            ghT = sb("ghT", [128, 32 * 512], BF16, p3)
            y_sb = [sb("y_sb%d" % i, [128, C], F32, p3) for i in range(2)]
            ps_h = [ps("ps_h%d" % i, [128, 512], F32, p3) for i in range(2)]
            ps_y = [ps("ps_y%d" % i, [128, C], F32, p3) for i in range(2)]
            ps_trf = ps("ps_trf", [128, C], BF16, p3)

            def _(e):
                e.memset(y_sb[0][:, :], 0.0)
                inst = e.memset(y_sb[1][:, :], 0.0)
                sem.incm(inst, "vS", "yzero")
            on("vector", _)

            def _(e):
                sem.wait(e, "yzero")
                for i in range(16):
                    sem.dma(e, "qZ", g_scr[i * 128:(i + 1) * 128, :], y_sb[i % 2][:, :])
                sem.mark("gzero", "qZ")
            on("gpsimd", _)

            for ex in range(NE):
                def _(e, ex=ex):
                    if ex == 0:
                        sem.wait(e, "wcast")
                    for g in range(4):
                        gg = ex * 4 + g
                        if gg >= 2:
                            ex2, g2 = divmod(gg - 2, 4)
                            sem.wait(e, "l1g%d_%d" % (ex2, g2))
                        sem.dma(e, "qW%d" % (g % 2),
                                w1sb[g % 2][:, :].rearrange("p (kc m) -> p kc m", kc=8),
                                w1bf[ex, :, g * 1024:(g + 1) * 1024].rearrange(
                                    "(kc p) m -> p kc m", p=128),
                                mname="w1ld%d_%d" % (ex, g))
                    if ex >= 1:
                        sem.wait(e, "l2%d_3" % (ex - 1))
                    sem.dma(e, "qW2",
                            w2sb[:, :].rearrange("p (kc m) -> p kc m", kc=32),
                            w2bf[ex, :, :].rearrange("(kc p) m -> p kc m", p=128),
                            mname="w2ld%d" % ex)
                on("sync", _)

                def _(e, ex=ex):
                    if ex == 0:
                        sem.wait(e, "idx_sb")
                    else:
                        sem.wait(e, "xtrc%d" % (ex - 1))
                        sem.wait(e, "yev%d_3" % (ex - 1))
                    for j in range(4):
                        inst = e.indirect_dma_start(
                            out=xin_tm[:, j * C:(j + 1) * C], out_offset=None,
                            in_=xn_scr[:, :],
                            in_offset=IndirectOffsetOnAxis(
                                idx_sb[:, ex * 4 + j: ex * 4 + j + 1], 0),
                        )
                        sem.inc(inst, "qGa", 16)
                    for j in range(4):
                        inst = e.indirect_dma_start(
                            out=p_sb[:, j * NE:(j + 1) * NE], out_offset=None,
                            in_=probs_scr[:, :],
                            in_offset=IndirectOffsetOnAxis(
                                idx_sb[:, ex * 4 + j: ex * 4 + j + 1], 0),
                        )
                        sem.inc(inst, "qGa", 16)
                    sem.mark("gat%d" % ex, "qGa")
                on("gpsimd", _)

                def _(e, ex=ex):
                    sem.wait(e, "gat%d" % ex)
                    for j in range(4):
                        if j == 0:
                            if ex >= 1:
                                sem.wait(e, "xinTe%d_3" % (ex - 1))
                        else:
                            sem.wait(e, "xinTe%d_%d" % (ex, j - 1))
                        for fc in range(8):
                            inst = e.transpose(ps_trf[:, fc * 128:(fc + 1) * 128],
                                               xin_tm[:, j * C + fc * 128: j * C + (fc + 1) * 128],
                                               ident[:, :])
                        sem.incm(inst, "tS", "xtr%d_%d" % (ex, j))
                        if j == 3:
                            sem.mark("xtrc%d" % ex, "tS")
                    for mc in range(32):
                        g = mc // 8
                        if mc == 0:
                            sem.wait(e, "xinTe%d_3" % ex)
                        if mc % 8 == 0:
                            sem.wait(e, "w1ld%d_%d" % (ex, g))
                        gg = ex * 32 + mc
                        if gg >= 2:
                            ex2, mc2 = divmod(gg - 2, 32)
                            sem.wait(e, "gel%d_%d" % (ex2, mc2))
                        for kc in range(8):
                            inst = e.matmul(
                                ps_h[mc % 2][:, :],
                                w1sb[g % 2][:, kc * 1024 + (mc % 8) * 128:
                                            kc * 1024 + (mc % 8) * 128 + 128],
                                xinT[:, kc * 512:(kc + 1) * 512],
                                start=(kc == 0), stop=(kc == 7))
                        sem.incm(inst, "tS", "l1%d_%d" % (ex, mc))
                        if mc % 8 == 7:
                            sem.mark("l1g%d_%d" % (ex, g), "tS")
                    for mc2 in range(4):
                        gg = ex * 4 + mc2
                        if gg >= 2:
                            ex2, m2 = divmod(gg - 2, 4)
                            sem.wait(e, "yev%d_%d" % (ex2, m2))
                        for kc2 in range(32):
                            if kc2 == 0:
                                if mc2 == 0:
                                    sem.wait(e, "w2ld%d" % ex)
                                sem.wait(e, "gel%d_31" % ex)
                            for nh in range(2):
                                inst = e.matmul(
                                    ps_y[mc2 % 2][:, nh * 512:(nh + 1) * 512],
                                    ghT[:, kc2 * 512 + mc2 * 128: kc2 * 512 + mc2 * 128 + 128],
                                    w2sb[:, kc2 * 1024 + nh * 512:
                                         kc2 * 1024 + (nh + 1) * 512],
                                    start=(kc2 == 0), stop=(kc2 == 31))
                        sem.incm(inst, "tS", "l2%d_%d" % (ex, mc2))
                on("tensor", _)

                def _(e, ex=ex):
                    for j in range(4):
                        sem.wait(e, "xtr%d_%d" % (ex, j))
                        inst = e.tensor_copy(
                            xinT[:, :].rearrange("p (kc t) -> p kc t", kc=8)[:, :, j * 128:(j + 1) * 128],
                            ps_trf[:, :].rearrange("p (fc t) -> p fc t", t=128))
                        sem.incm(inst, "vS", "xinTe%d_%d" % (ex, j))
                on("vector", _)

                def _(e, ex=ex):
                    for mc in range(32):
                        sem.wait(e, "l1%d_%d" % (ex, mc))
                        inst = e.activation(ghT[:, mc * 512:(mc + 1) * 512],
                                            ps_h[mc % 2][:, :], AF.Gelu)
                        sem.incm(inst, "sS", "gel%d_%d" % (ex, mc))
                    for mc2 in range(4):
                        sem.wait(e, "l2%d_%d" % (ex, mc2))
                        gg = ex * 4 + mc2
                        if gg >= 2:
                            ex2, m2 = divmod(gg - 2, 4)
                            sem.wait(e, "sca%d_%d" % (ex2, m2))
                        if ex == 0 and mc2 == 0:
                            sem.wait(e, "gat0")
                        inst = e.activation(y_sb[mc2 % 2][:, :], ps_y[mc2 % 2][:, :],
                                            AF.Copy,
                                            scale=p_sb[:, mc2 * NE + ex: mc2 * NE + ex + 1])
                        sem.incm(inst, "sS", "yev%d_%d" % (ex, mc2))
                on("scalar", _)

                def _(e, ex=ex):
                    for mc2 in range(4):
                        sem.wait(e, "yev%d_%d" % (ex, mc2))
                        if ex == 0 and mc2 == 0:
                            sem.wait(e, "gzero")
                        inst = e.indirect_dma_start(
                            out=g_scr[:, :],
                            out_offset=IndirectOffsetOnAxis(
                                idx_sb[:, ex * 4 + mc2: ex * 4 + mc2 + 1], 0),
                            in_=y_sb[mc2 % 2][:, :],
                            in_offset=None,
                            compute_op=ALU.add,
                        )
                        sem.inc(inst, "qS%d" % (mc2 % 2), 16)
                        sem.mark("sca%d_%d" % (ex, mc2), "qS%d" % (mc2 % 2))
                    sem.waitq(e, "qS0")
                    sem.waitq(e, "qS1")
                on("gpsimd", _)

            if DEBUG >= 1:
                d_g = dbgout("g", [T, C], F32)

                def _(e):
                    sem.waitq(e, "qS0")
                    sem.waitq(e, "qS1")
                    for i in range(16):
                        sem.dma(e, "qD", d_g[i * 128:(i + 1) * 128, :],
                                g_scr[i * 128:(i + 1) * 128, :])
                on("sync", _)

            fence("p3_end")
            p3.close()

        if PHASES >= 7:
            p4 = ExitStack()
            gt = [sb("gt%d" % i, [128, 2 * C], F32, p4) for i in range(3)]
            xt4 = [sb("xt4%d" % i, [128, 2 * C], F32, p4) for i in range(3)]
            st4 = [sb("st4%d" % i, [128, 2 * C], F32, p4) for i in range(3)]
            sg4 = sb("sg4", [128, 2 * C], F32, p4)
            o4 = sb("o4", [128, 2 * C], F32, p4)

            for i in range(8):
                def _(e, i=i):
                    if i >= 3:
                        sem.wait(e, "cmb%d" % (i - 3))
                    qc = "qC%d" % (i % 3)
                    sem.dma(e, qc, gt[i % 3][:, :].rearrange("p (j c) -> p j c", c=C),
                            g_scr[i * 256:(i + 1) * 256, :].rearrange("(j p) c -> p j c", p=128))
                    sem.dma(e, qc, xt4[i % 3][:, :].rearrange("p (j c) -> p j c", c=C),
                            x_tm[i * 256:(i + 1) * 256, :].rearrange("(j p) c -> p j c", p=128))
                    sem.dma(e, qc, st4[i % 3][:, :].rearrange("p (j c) -> p j c", c=C),
                            s_scr[i * 256:(i + 1) * 256, :].rearrange("(j p) c -> p j c", p=128),
                            mname="cld%d" % i)
                on("sync", _)

                def _(e, i=i):
                    sem.wait(e, "cld%d" % i)
                    if i >= 1:
                        sem.wait(e, "cmb%d" % (i - 1))
                    inst = e.activation(sg4[:, :], gt[i % 3][:, :], AF.Silu)
                    sem.incm(inst, "sS", "sil%d" % i)
                on("scalar", _)

                def _(e, i=i):
                    sem.wait(e, "sil%d" % i)
                    if i >= 1:
                        sem.wait(e, "yout%d" % (i - 1))
                    sem.vself(e, e.tensor_mul(sg4[:, :], sg4[:, :], st4[i % 3][:, :]))
                    inst = e.tensor_add(o4[:, :], sg4[:, :], xt4[i % 3][:, :])
                    sem.incm(inst, "vS", "cmb%d" % i)
                on("vector", _)

                def _(e, i=i):
                    sem.wait(e, "cmb%d" % i)
                    sem.dma(e, "qY",
                            y_d[i * 256:(i + 1) * 256, :].rearrange("(j p) c -> p j c", p=128),
                            o4[:, :].rearrange("p (j c) -> p j c", c=C),
                            mname="yout%d" % i)
                on("sync", _)
            p4.close()

        def _(e):
            for smn in list(sem.h.keys()):
                if smn.startswith("q"):
                    sem.waitq(e, smn)
        on("sync", _)

    gctx.close()
    return nc, dbg, sem.marks


def make_common(inp):
    m = {}
    m["srcT"] = np.ascontiguousarray(inp["L0_attn_source_embedding"].T)
    m["valT"] = np.ascontiguousarray(inp["L0_attn_value_embedding"].T)
    m["Wq"] = np.ascontiguousarray(inp["L0_attn_Wq"])
    m["Wk"] = np.ascontiguousarray(inp["L0_attn_Wk"])
    m["Wv"] = np.ascontiguousarray(inp["L0_attn_Wv"])
    m["Wo"] = np.ascontiguousarray(inp["L0_attn_Wo"])
    m["biases"] = np.ascontiguousarray(np.concatenate([
        inp["L0_attn_bq"], inp["L0_attn_bk"], inp["L0_attn_bv"], inp["L0_attn_bo"],
        inp["L0_attn_ln_g"], inp["L0_attn_ln_b"]]).astype(np.float32)).reshape(1, 6 * C)
    m["gate_w"] = np.ascontiguousarray(inp["L0_ffn_gate_w"])
    m["w1"] = np.ascontiguousarray(inp["L0_ffn_w1"])
    m["w2"] = np.ascontiguousarray(inp["L0_ffn_w2"])
    m["ident"] = np.eye(128).astype(ml_dtypes.bfloat16)
    m["onesf"] = np.ones((1, 128), np.float32)
    m["onescf"] = np.ones((128, 1), np.float32)
    sel = np.zeros((16, 16 * 128), np.float32)
    for i in range(16):
        sel[i, i * 128:(i + 1) * 128] = 1.0
    m["sel16"] = sel.astype(ml_dtypes.bfloat16)
    m["iota8"] = np.broadcast_to(np.arange(T, dtype=np.float16), (NE, T)).copy()
    cst = np.zeros((128, 8), np.float32)
    cst[:, 0] = -1.0
    cst[:, 1] = 1e-5
    cst[:, 2] = 1e-12
    cst[:, 3] = 0.125
    cst[:, 4] = -0.5
    m["consts"] = cst
    return m


def kernel(**inputs):
    _nc1, _dbg1, marks = build_nc()
    nc, dbg, _ = build_nc(future=marks)
    common = make_common(inputs)
    x = inputs["L0_attn_target_embedding"]
    in_maps = []
    n_cores = int(os.environ.get("DEV_CORES", "8"))
    for b in range(n_cores):
        mm = dict(common)
        mm["x_tm"] = np.ascontiguousarray(x[b])
        mm["xT"] = np.ascontiguousarray(x[b].T)
        in_maps.append(mm)
    res = run_bass_kernel_spmd(nc, in_maps, core_ids=list(range(n_cores)))
    if DEBUG:
        return res
    return np.stack([r["y"] for r in res.results], 0)


# version 12
# speedup vs baseline: 1.1616x; 1.1616x over previous
import os
import numpy as np
import ml_dtypes
import concourse.bass as bass
import concourse.mybir as mybir
from concourse.bass import IndirectOffsetOnAxis
from concourse.bass_utils import run_bass_kernel_spmd
from contextlib import ExitStack

F32 = mybir.dt.float32
BF16 = mybir.dt.bfloat16
I32 = mybir.dt.int32
F16 = mybir.dt.float16
AF = mybir.ActivationFunctionType
ALU = mybir.AluOpType
AX = mybir.AxisListType

T = 2048
C = 1024
S = 2048
NH = 16
E = 64
HF = 4096
NE = 8
CAP = 512
NBIS = 26
BIG = 30000.0

PHASES = int(os.environ.get("DEV_PHASES", "9"))
DEBUG = int(os.environ.get("DEV_DEBUG", "0"))

ENGS = ("sync", "tensor", "vector", "scalar", "gpsimd")
SEMS = ("qS", "qG", "tS", "vS", "sS")


class Env:
    def __init__(self, nc, ctx, future=None):
        self.nc = nc
        self.ctx = ctx
        self.h = {}
        self.c = {}
        self.marks = {}
        self.future = future
        for n in SEMS:
            self.get(n)

    def get(self, n):
        if n not in self.h:
            self.h[n] = self.ctx.enter_context(self.nc.semaphore(n))
            self.c[n] = 0
        return self.h[n]

    def inc(self, inst, name, k=1):
        inst.then_inc(self.get(name), k)
        self.c[name] += k

    def mark(self, name, sm):
        self.marks[name] = (sm, self.c[sm])

    def incm(self, inst, sm, mname, k=1):
        self.inc(inst, sm, k)
        self.mark(mname, sm)

    def dma(self, eng, q, out, in_, mname=None, **kw):
        inst = eng.dma_start(out=out, in_=in_, **kw)
        self.inc(inst, q, 16)
        if mname:
            self.mark(mname, q)

    def wait(self, eng, mname):
        if mname in self.marks:
            sm, v = self.marks[mname]
        elif self.future is not None:
            sm, v = self.future[mname]
        else:
            return
        if v > 0:
            eng.wait_ge(self.get(sm), v)

    def vself(self, eng, inst):
        self.inc(inst, "vS", 1)
        eng.wait_ge(self.h["vS"], self.c["vS"])

    def waitq(self, eng, q):
        if self.c.get(q, 0) > 0:
            eng.wait_ge(self.h[q], self.c[q])


ADDRS = {}

def build_nc(future=None):
    ADDRS.clear()
    nc = bass.Bass("TRN2", target_bir_lowering=False)
    dt = nc.dram_tensor

    x_tm = dt("x_tm", [T, C], F32, kind="ExternalInput").ap()
    xT_d = dt("xT", [C, T], F32, kind="ExternalInput").ap()
    srcT_d = dt("srcT", [C, S], F32, kind="ExternalInput").ap()
    valT_d = dt("valT", [C, S], F32, kind="ExternalInput").ap()
    Wq_d = dt("Wq", [C, C], F32, kind="ExternalInput").ap()
    Wk_d = dt("Wk", [C, C], F32, kind="ExternalInput").ap()
    Wv_d = dt("Wv", [C, C], F32, kind="ExternalInput").ap()
    Wo_d = dt("Wo", [C, C], F32, kind="ExternalInput").ap()
    biases_d = dt("biases", [1, 6 * C], F32, kind="ExternalInput").ap()
    gate_d = dt("gate_w", [C, NE], F32, kind="ExternalInput").ap()
    w1_d = dt("w1", [NE, C, HF], F32, kind="ExternalInput").ap()
    w2_d = dt("w2", [NE, HF, C], F32, kind="ExternalInput").ap()
    ident_d = dt("ident", [128, 128], BF16, kind="ExternalInput").ap()
    onesf_d = dt("onesf", [1, 128], F32, kind="ExternalInput").ap()
    onescf_d = dt("onescf", [128, 1], F32, kind="ExternalInput").ap()
    sel16_d = dt("sel16", [16, 16 * 128], BF16, kind="ExternalInput").ap()
    iota8_d = dt("iota8", [NE, T], F16, kind="ExternalInput").ap()
    consts_d = dt("consts", [128, 8], F32, kind="ExternalInput").ap()

    xn_scr = dt("xn_scr", [T, C], BF16, kind="Internal").ap()
    probs_scr = dt("probs_scr", [T, NE], F32, kind="Internal").ap()
    idx_scr = dt("idx_scr", [NE * CAP, 1], I32, kind="Internal").ap()
    g_scr = dt("g_scr", [T, C], F32, kind="Internal").ap()
    w1bf = dt("w1bf", [NE, C, HF], BF16, kind="Internal").ap()
    w2bf = dt("w2bf", [NE, HF, C], BF16, kind="Internal").ap()
    s_scr = dt("s_scr", [T, C], F32, kind="Internal").ap()
    y_d = dt("y", [T, C], F32, kind="ExternalOutput").ap()

    dbg = {}

    def dbgout(name, shape, dty):
        dbg[name] = dt("dbg_" + name, shape, dty, kind="ExternalOutput").ap()
        return dbg[name]

    gctx = ExitStack()

    def sb(name, shape, dty, ctx=None):
        pre = nc.sbuf_base
        h = (ctx or gctx).enter_context(nc.sbuf_tensor(name, shape, dty))
        ADDRS[name] = (pre, nc.sbuf_base)
        return h

    def ps(name, shape, dty, ctx=None):
        return (ctx or gctx).enter_context(nc.psum_tensor(name, shape, dty))

    sem = Env(nc, gctx, future)

    ident = sb("ident_s", [128, 128], BF16)
    onesf = sb("onesf_s", [1, 128], F32)
    onescf = sb("onescf_s", [128, 1], F32)
    consts = sb("consts_s", [128, 8], F32)
    nc.const_aps.register(F32, -1.0, consts[:, 0:1])
    nc.const_aps.register(F32, 1e-5, consts[:, 1:2])
    nc.const_aps.register(F32, 1e-12, consts[:, 2:3])
    nc.const_aps.register(F32, 0.125, consts[:, 3:4])
    nc.const_aps.register(F32, -0.5, consts[:, 4:5])
    nc.const_aps.register(F32, -1.0, consts[:, 5:6]) if False else None
    bias_bc = sb("bias_bc", [128, C], F32)
    BCX = [None]

    def BC(j):
        if j == 0:
            return bias_bc[:, :]
        return BCX[0][:, (j - 1) * C: j * C]

    stats = sb("stats", [128, 32], F32)
    gwsb = sb("gwsb", [128, 8 * NE], F32)
    idx_sb = sb("idx_sb", [128, NE * 4], I32)
    identf = sb("identf", [128, 128], F32)
    identh = sb("identh", [16, 16], F16)
    lo = sb("lo", [NE, 1], F32)
    hi = sb("hi", [NE, 1], F32)
    mid = sb("mid", [NE, 1], F32)
    cnt = sb("cnt", [NE, 1], F32)
    bm = sb("bm", [NE, 1], F32)
    msku8 = sb("msku8", [NE, 1], mybir.dt.uint8)
    c1 = sb("c1", [NE, 1], F32)
    takeb = sb("takeb", [NE, 1], F32)

    B = [None]

    def on(eng_name, f):
        getattr(B[0], eng_name)(f)

    def fence(tag):
        for eng_name in ENGS:
            def _(e):
                for smn in list(sem.h.keys()):
                    if sem.c[smn] > 0:
                        e.wait_ge(sem.h[smn], sem.c[smn])
            on(eng_name, _)

    pKV = ExitStack()
    knT = sb("knT", [128, 8 * S], BF16, pKV)
    v_tm = sb("v_tm", [128, 16 * 1040], BF16, pKV)

    with nc.Block() as block:
        B[0] = block

        p0 = ExitStack()
        srcT_bf = sb("srcT_bf", [128, 8 * S], BF16, p0)
        valT_bf = sb("valT_bf", [128, 8 * S], BF16, p0)
        Wk_bf = sb("Wk_bf", [128, 8 * C], BF16, p0)
        Wv_bf = sb("Wv_bf", [128, 8 * C], BF16, p0)
        eA = sb("eA", [128, C], F32, p0)
        eB = sb("eB", [128, C], F32, p0)
        eC = sb("eC", [128, C], F32, p0)
        eD = sb("eD", [128, 16], F32, p0)
        eE = sb("eE", [128, 16], F32, p0)
        eF = sb("eF", [128, 16], F32, p0)
        kn_bf = sb("kn_bf", [128, C], BF16, p0)
        bkv_bc = sb("bkv_bc", [128, 2 * C], F32, p0)
        ps_kv = ps("ps_kv", [128, C], F32, p0)
        ps_kv2 = ps("ps_kv2", [128, C], F32, p0)
        ps_tr0 = ps("ps_tr0", [128, C], BF16, p0)

        def _(e):
            sem.dma(e, "qS", ident[:], ident_d[:, :])
            sem.dma(e, "qS", onesf[:], onesf_d[:, :])
            sem.dma(e, "qS", onescf[:], onescf_d[:, :])
            sem.dma(e, "qS", consts[:], consts_d[:, :])
            sem.dma(e, "qS", BC(0), biases_d[0:1, 0:C].broadcast_to([128, C]))
            for jj, j in enumerate((1, 2)):
                sem.dma(e, "qS", bkv_bc[:, jj * C:(jj + 1) * C],
                        biases_d[0:1, j * C:(j + 1) * C].broadcast_to([128, C]))
            sem.dma(e, "qS", gwsb[:].rearrange("p (kc e) -> p kc e", kc=8),
                    gate_d.rearrange("(kc p) e -> p kc e", p=128),
                    mname="ld_consts")
        on("sync", _)

        def _(e):
            sem.dma(e, "qG", srcT_bf[:].rearrange("p (kc s) -> p kc s", kc=8),
                    srcT_d.rearrange("(kc p) s -> p kc s", p=128))
            sem.dma(e, "qG", valT_bf[:].rearrange("p (kc s) -> p kc s", kc=8),
                    valT_d.rearrange("(kc p) s -> p kc s", p=128))
            sem.dma(e, "qG", Wk_bf[:].rearrange("p (kc a) -> p kc a", kc=8),
                    Wk_d.rearrange("(kc p) a -> p kc a", p=128))
            sem.dma(e, "qG", Wv_bf[:].rearrange("p (kc a) -> p kc a", kc=8),
                    Wv_d.rearrange("(kc p) a -> p kc a", p=128),
                    mname="ld_kv")
        on("gpsimd", _)

        def _(e):
            sem.wait(e, "ld_consts")
            e.tensor_copy(identf[:, :], ident[:, :])
            inst = e.tensor_copy(identh[:, :], ident[0:16, 0:16])
            sem.incm(inst, "vS", "identf")
        on("vector", _)

        def _(e):
            inst = e.memset(v_tm[:, :], 1.0)
            sem.incm(inst, "vS", "v_ones")
        on("vector", _)

        for sc in range(16):
            def _(e, sc=sc):
                if sc == 0:
                    sem.wait(e, "ld_kv")
                else:
                    sem.wait(e, "kadd%d" % (sc - 1))
                for kc in range(8):
                    for nh in range(2):
                        inst = e.matmul(ps_kv[:, nh * 512:(nh + 1) * 512],
                                        srcT_bf[:, kc * S + sc * 128: kc * S + sc * 128 + 128],
                                        Wk_bf[:, kc * C + nh * 512: kc * C + (nh + 1) * 512],
                                        start=(kc == 0), stop=(kc == 7))
                sem.incm(inst, "tS", "kmm%d" % sc)
                if sc >= 1:
                    sem.wait(e, "vadd%d" % (sc - 1))
                for kc in range(8):
                    for nh in range(2):
                        inst = e.matmul(ps_kv2[:, nh * 512:(nh + 1) * 512],
                                        valT_bf[:, kc * S + sc * 128: kc * S + sc * 128 + 128],
                                        Wv_bf[:, kc * C + nh * 512: kc * C + (nh + 1) * 512],
                                        start=(kc == 0), stop=(kc == 7))
                sem.incm(inst, "tS", "vmm%d" % sc)
            on("tensor", _)

            def _(e, sc=sc):
                sem.wait(e, "kmm%d" % sc)
                if sc == 0:
                    sem.wait(e, "ld_consts")
                inst = e.tensor_add(eA[:, :], ps_kv[:, :], bkv_bc[:, 0:C])
                sem.incm(inst, "vS", "kadd%d" % sc)
                e.wait_ge(sem.h["vS"], sem.c["vS"])
                inst = e.tensor_scalar_min(eB[:, :], eA[:, :], 0.0)
                sem.incm(inst, "vS", "kmin%d" % sc)
            on("vector", _)

            def _(e, sc=sc):
                sem.wait(e, "kmin%d" % sc)
                inst = e.activation(eC[:, :], eB[:, :], AF.Exp)
                sem.incm(inst, "sS", "kexp%d" % sc)
            on("scalar", _)

            def _(e, sc=sc):
                sem.wait(e, "kexp%d" % sc)
                sem.vself(e, e.scalar_tensor_tensor(eB[:, :], eA[:, :], 0.0, eC[:, :],
                                                    ALU.max, ALU.add))
                sem.vself(e, e.tensor_scalar_add(eB[:, :], eB[:, :], -1.0))
                sem.vself(e, e.tensor_mul(eC[:, :], eB[:, :], eB[:, :]))
                inst = e.tensor_reduce(eD[:, :], eC[:, :].rearrange("p (a b) -> p a b", b=64),
                                       AX.X, ALU.add)
                sem.incm(inst, "vS", "kss%d" % sc)
            on("vector", _)

            def _(e, sc=sc):
                sem.wait(e, "kss%d" % sc)
                inst = e.activation(eE[:, :], eD[:, :], AF.Ln)
                sem.incm(inst, "sS", "kln%d" % sc)
                inst = e.activation(eF[:, :], eE[:, :], AF.Exp, scale=-0.5)
                sem.incm(inst, "sS", "ksrt%d" % sc)
            on("scalar", _)

            def _(e, sc=sc):
                sem.wait(e, "ksrt%d" % sc)
                if sc >= 1:
                    sem.wait(e, "ktr%d" % (sc - 1))
                for h in range(16):
                    inst = e.tensor_scalar(kn_bf[:, h * 64:(h + 1) * 64],
                                           eB[:, h * 64:(h + 1) * 64],
                                           eF[:, h:h + 1], None, ALU.mult)
                sem.incm(inst, "vS", "kn%d" % sc)
            on("vector", _)

            def _(e, sc=sc):
                sem.wait(e, "kn%d" % sc)
                if sc == 0:
                    sem.wait(e, "ld_consts")
                if sc >= 1:
                    sem.wait(e, "ktrc%d" % (sc - 1))
                for fc in range(8):
                    inst = e.transpose(ps_tr0[:, fc * 128:(fc + 1) * 128],
                                       kn_bf[:, fc * 128:(fc + 1) * 128], ident[:, :])
                sem.incm(inst, "tS", "ktr%d" % sc)
            on("tensor", _)

            def _(e, sc=sc):
                sem.wait(e, "ktr%d" % sc)
                inst = e.tensor_copy(
                    knT[:, :].rearrange("p (fc s) -> p fc s", fc=8)[:, :, sc * 128:(sc + 1) * 128],
                    ps_tr0[:, :].rearrange("p (fc t) -> p fc t", t=128))
                sem.incm(inst, "vS", "ktrc%d" % sc)
                sem.wait(e, "vmm%d" % sc)
                if sc == 0:
                    sem.wait(e, "v_ones")
                inst = e.tensor_add(eA[:, :], ps_kv2[:, :], bkv_bc[:, C:2 * C])
                sem.incm(inst, "vS", "vadd%d" % sc)
                inst = e.tensor_copy(
                    v_tm[:, :].rearrange("p (sc h x) -> p sc h x", sc=16, x=65)[:, sc, :, 0:64],
                    eA[:, :].rearrange("p (h x) -> p h x", x=64))
                sem.incm(inst, "vS", "vcp%d" % sc)
            on("vector", _)

        if DEBUG >= 1 and PHASES >= 1:
            d_knT = dbgout("knT", [128, 8 * S], F32)
            d_v = dbgout("v", [128, 16 * 1040], F32)

            def _(e):
                sem.wait(e, "ktrc15")
                sem.wait(e, "vcp15")
                sem.dma(e, "qD", d_knT[:, :], knT[:, :])
                sem.dma(e, "qD", d_v[:, :], v_tm[:, :])
            on("gpsimd", _)

        fence("p0_end")
        p0.close()

        pQ = ExitStack()
        qnT = sb("qnT", [128, 16 * C], BF16, pQ)
        pA = ExitStack()
        xnT = sb("xnT", [128, 8 * T], BF16, pA)

        if PHASES >= 2:
            pR = ExitStack()
            probsT = sb("probsT", [NE, T], F32, pR)
            cmpw = sb("cmpw", [NE, T], F16, pR)
            p1 = ExitStack()
            tmp1 = sb("tmp1", [128, 1], F32, p1)
            tmp2 = sb("tmp2", [128, 1], F32, p1)
            tmp3 = sb("tmp3", [128, 1], F32, p1)
            sd1 = sb("sd1", [128, 1], F32, p1)
            negcs = sb("negcs", [NE, 1], F32, p1)
            p1m3 = ExitStack()
            mu_bc = sb("mu_bc", [128, T], F32, p1m3)
            rstd_bc = sb("rstd_bc", [128, T], F32, p1m3)
            p1m1 = ExitStack()
            xtile = [sb("xtile%d" % i, [128, C], F32, p1m1) for i in range(2)]
            x2t = sb("x2t", [128, C], F32, p1m1)
            xn_t = sb("xn_t", [128, C], BF16, p1m1)

            for i in range(16):
                def _(e, i=i):
                    if i >= 2:
                        sem.wait(e, "xcons%d" % (i - 2))
                    sem.dma(e, "qX%d" % (i % 2), xtile[i % 2][:, :],
                            x_tm[i * 128:(i + 1) * 128, :], mname="xld%d" % i)
                on("sync", _)

                def _(e, i=i):
                    sem.wait(e, "xld%d" % i)
                    if i >= 1:
                        sem.wait(e, "xvar%d" % (i - 1))
                    inst = e.activation(x2t[:, :], xtile[i % 2][:, :], AF.Square)
                    sem.incm(inst, "sS", "xsq%d" % i)
                on("scalar", _)

                def _(e, i=i):
                    sem.wait(e, "xld%d" % i)
                    sem.vself(e, e.tensor_reduce(tmp1[:, :], xtile[i % 2][:, :],
                                                 AX.X, ALU.add))
                    sem.wait(e, "xsq%d" % i)
                    sem.vself(e, e.tensor_reduce(tmp2[:, :], x2t[:, :], AX.X, ALU.add))
                    sem.vself(e, e.tensor_scalar_mul(stats[:, i:i + 1], tmp1[:, :],
                                                     1.0 / C))
                    sem.vself(e, e.tensor_scalar_mul(tmp2[:, :], tmp2[:, :], 1.0 / C))
                    sem.vself(e, e.tensor_mul(tmp3[:, :], stats[:, i:i + 1],
                                              stats[:, i:i + 1]))
                    inst = e.tensor_sub(tmp2[:, :], tmp2[:, :], tmp3[:, :])
                    sem.incm(inst, "vS", "xvar%d" % i)
                on("vector", _)

                def _(e, i=i):
                    sem.wait(e, "xvar%d" % i)
                    inst = e.activation(sd1[:, :], tmp2[:, :], AF.Sqrt, bias=1e-5)
                    sem.incm(inst, "sS", "xsd%d" % i)
                on("scalar", _)

                def _(e, i=i):
                    sem.wait(e, "xsd%d" % i)
                    sem.vself(e, e.reciprocal(stats[:, 16 + i:17 + i], sd1[:, :]))
                    if i >= 1:
                        sem.wait(e, "xnst%d" % (i - 1))
                    inst = e.tensor_scalar(xn_t[:, :], xtile[i % 2][:, :],
                                           stats[:, i:i + 1], stats[:, 16 + i:17 + i],
                                           ALU.subtract, ALU.mult)
                    sem.incm(inst, "vS", "xcons%d" % i)
                on("vector", _)

                def _(e, i=i):
                    sem.wait(e, "xcons%d" % i)
                    sem.dma(e, "qXn", xn_scr[i * 128:(i + 1) * 128, :], xn_t[:, :],
                            mname="xnst%d" % i)
                on("gpsimd", _)

            p1s1 = ExitStack()
            ps_st = ps("ps_st", [16, 128], BF16, p1s1)
            ps_st2 = ps("ps_st2", [16, 128], BF16, p1s1)
            p1m1.close()
            p1m2 = ExitStack()
            stats_bf = sb("stats_bf", [128, 32], BF16, p1m2)
            statsT = sb("statsT", [16, 128], BF16, p1m2)
            statsT2 = sb("statsT2", [16, 128], BF16, p1m2)
            sel16s = sb("sel16_s", [16, 16 * 128], BF16, p1m2)
            ps_mu = ps("ps_mu", [128, 512], F32, p1s1)
            ps_mu2 = ps("ps_mu2", [128, 512], F32, p1s1)

            def _(e):
                sem.wait(e, "xcons15")
                sem.wait(e, "xnst15")
                sem.dma(e, "qS", sel16s[:], sel16_d[:, :], mname="ld_sel16")
            on("sync", _)

            def _(e):
                sem.wait(e, "xcons15")
                inst = e.tensor_copy(stats_bf[:, :], stats[:, :])
                sem.incm(inst, "vS", "stats_bf")
            on("vector", _)

            def _(e):
                sem.wait(e, "stats_bf")
                e.transpose(ps_st[:, :], stats_bf[:, 0:16], ident[:, :])
                inst = e.transpose(ps_st2[:, :], stats_bf[:, 16:32], ident[:, :])
                sem.incm(inst, "tS", "sttr")
            on("tensor", _)

            def _(e):
                sem.wait(e, "sttr")
                e.tensor_copy(statsT[:, :], ps_st[:, :])
                inst = e.tensor_copy(statsT2[:, :], ps_st2[:, :])
                sem.incm(inst, "vS", "statsT")
            on("vector", _)

            for grp in range(4):
                def _(e, grp=grp):
                    sem.wait(e, "statsT")
                    sem.wait(e, "ld_sel16")
                    if grp >= 1:
                        sem.wait(e, "mubc%d" % (grp - 1))
                    for q in range(4):
                        i = grp * 4 + q
                        e.matmul(ps_mu[:, q * 128:(q + 1) * 128],
                                 sel16s[:, i * 128:(i + 1) * 128], statsT[:, :],
                                 start=True, stop=True)
                        inst = e.matmul(ps_mu2[:, q * 128:(q + 1) * 128],
                                        sel16s[:, i * 128:(i + 1) * 128], statsT2[:, :],
                                        start=True, stop=True)
                    sem.incm(inst, "tS", "mumm%d" % grp)
                on("tensor", _)

                def _(e, grp=grp):
                    sem.wait(e, "mumm%d" % grp)
                    e.tensor_copy(mu_bc[:, grp * 512:(grp + 1) * 512], ps_mu[:, :])
                    inst = e.tensor_copy(rstd_bc[:, grp * 512:(grp + 1) * 512], ps_mu2[:, :])
                    sem.incm(inst, "vS", "mubc%d" % grp)
                on("vector", _)

            p1s1.close()
            p1m2.close()
            p1m4 = ExitStack()
            xTt = [sb("xTt%d" % i, [128, T // 2], F32, p1m4) for i in range(2)]
            t1w = sb("t1w", [128, T // 2], F32, p1m4)
            p1s2 = ExitStack()
            ps_g = ps("ps_g", [NE, T], F32, p1s2)

            for it2 in range(16):
                kc, hf = it2 // 2, it2 % 2
                def _(e, it2=it2, kc=kc, hf=hf):
                    if it2 == 0:
                        sem.wait(e, "xcons15")
                        sem.wait(e, "xnst15")
                        sem.wait(e, "mumm3")
                    if it2 >= 2:
                        sem.wait(e, "xnTc%d" % (it2 - 2))
                        sem.wait(e, "gmm%d" % (it2 - 2))
                    sem.dma(e, "qT%d" % (it2 % 2), xTt[it2 % 2][:, :],
                            xT_d[kc * 128:(kc + 1) * 128, hf * 1024:(hf + 1) * 1024],
                            mname="xTld%d" % it2)
                on("sync", _)

                def _(e, it2=it2, kc=kc, hf=hf):
                    sem.wait(e, "xTld%d" % it2)
                    if it2 == 0:
                        sem.wait(e, "mubc3")
                    for q in range(2):
                        inst = e.matmul(ps_g[:, (hf * 2 + q) * 512:(hf * 2 + q + 1) * 512],
                                        gwsb[:, kc * NE:(kc + 1) * NE],
                                        xTt[it2 % 2][:, q * 512:(q + 1) * 512],
                                        start=(kc == 0), stop=(kc == 7))
                    sem.incm(inst, "tS", "gmm%d" % it2)
                on("tensor", _)

                def _(e, it2=it2, kc=kc, hf=hf):
                    sem.wait(e, "xTld%d" % it2)
                    sem.wait(e, "mubc3")
                    e.tensor_sub(t1w[:, :], xTt[it2 % 2][:, :],
                                 mu_bc[:, hf * 1024:(hf + 1) * 1024])
                    inst = e.tensor_mul(xnT[:, kc * T + hf * 1024: kc * T + (hf + 1) * 1024],
                                        t1w[:, :], rstd_bc[:, hf * 1024:(hf + 1) * 1024])
                    sem.incm(inst, "vS", "xnTc%d" % it2)
                on("vector", _)

            ps_cs = ps("ps_cs", [1, NE], F32, p1s2)
            ps_cst = ps("ps_cst", [NE, 128], F32, p1s2)
            p1m5 = ExitStack()
            gex = sb("gex", [NE, T], F32, p1m5)
            grecip = sb("grecip", [1, T], F16, p1m5)
            obf8 = sb("obf8", [1, NE], BF16, p1m5)

            def _(e):
                inst = e.memset(obf8[:, :], 1.0)
                sem.incm(inst, "vS", "obf8")
            on("vector", _)

            def _(e):
                sem.wait(e, "ld_consts")
                for kc in range(8):
                    inst = e.matmul(ps_cst[:, 0:1], gwsb[:, kc * NE:(kc + 1) * NE],
                                    onescf[:, 0:1],
                                    start=(kc == 0), stop=(kc == 7))
                sem.incm(inst, "tS", "cstr")
            on("tensor", _)

            def _(e):
                sem.wait(e, "cstr")
                inst = e.tensor_scalar_mul(negcs[:, :], ps_cst[:, 0:1], -1.0)
                sem.incm(inst, "vS", "negcs")
                e.wait_ge(sem.h["vS"], sem.c["vS"])
                sem.wait(e, "gmm15")
                sem.vself(e, e.scalar_tensor_tensor(gex[:, :], mu_bc[0:NE, :],
                                                    negcs[:, 0:1], ps_g[:, :],
                                                    ALU.mult, ALU.add))
                inst = e.tensor_mul(gex[:, :], gex[:, :], rstd_bc[0:NE, :])
                sem.incm(inst, "vS", "gates")
            on("vector", _)

            def _(e):
                sem.wait(e, "gates")
                inst = e.activation(gex[:, :], gex[:, :], AF.Exp)
                sem.incm(inst, "sS", "gexp")
            on("scalar", _)
            p1s2.close()
            p1s3 = ExitStack()
            ps_sum = ps("ps_sum", [1, T], F32, p1s3)
            ps_rb8 = ps("ps_rb8", [NE, T], F32, p1s3)

            def _(e):
                sem.wait(e, "gexp")
                for q in range(4):
                    inst = e.matmul(ps_sum[:, q * 512:(q + 1) * 512],
                                    onescf[0:NE, 0:1],
                                    gex[:, q * 512:(q + 1) * 512],
                                    start=True, stop=True)
                sem.incm(inst, "tS", "gsum")
            on("tensor", _)

            def _(e):
                sem.wait(e, "gsum")
                inst = e.activation(grecip[:, :], ps_sum[:, :], AF.Ln)
                sem.incm(inst, "sS", "gln")
                inst = e.activation(grecip[:, :], grecip[:, :], AF.Exp, scale=-1.0)
                sem.incm(inst, "sS", "grecip")
            on("scalar", _)

            def _(e):
                sem.wait(e, "grecip")
                sem.wait(e, "obf8")
                for q in range(4):
                    inst = e.matmul(ps_rb8[:, q * 512:(q + 1) * 512],
                                    obf8[0:1, :], grecip[:, q * 512:(q + 1) * 512],
                                    start=True, stop=True)
                sem.incm(inst, "tS", "grb")
            on("tensor", _)

            def _(e):
                sem.wait(e, "grb")
                inst = e.tensor_mul(probsT[:, :], gex[:, :], ps_rb8[:, :])
                sem.incm(inst, "vS", "probsT")
            on("vector", _)

            p1s3.close()
            ps_ptm = ps("ps_ptm", [128, 128], F32, p1)
            ptm = sb("ptm", [128, 128], F32, p1m5)

            def _(e):
                sem.wait(e, "probsT")
                for cch in range(16):
                    inst = e.transpose(ps_ptm[:, cch * 8:(cch + 1) * 8],
                                       probsT[:, cch * 128:(cch + 1) * 128], identf[0:NE, 0:NE])
                sem.incm(inst, "tS", "ptm_tr")
            on("tensor", _)

            def _(e):
                sem.wait(e, "ptm_tr")
                inst = e.tensor_copy(ptm[:, :], ps_ptm[:, :])
                sem.incm(inst, "vS", "ptm")
            on("vector", _)

            def _(e):
                sem.wait(e, "ptm")
                sem.dma(e, "qPr",
                        probs_scr[:, :].rearrange("(cch p) e -> p cch e", p=128),
                        ptm[:, :].rearrange("p (cch e) -> p cch e", e=NE),
                        mname="probs_st")
            on("gpsimd", _)

            if DEBUG >= 1:
                d_xnT = dbgout("xnT", [128, 8 * T], F32)
                d_stats = dbgout("stats", [128, 32], F32)
                d_statsT = dbgout("statsT", [16, 2 * 128], F32)
                d_mubc = dbgout("mubc", [128, 2 * T], F32)
                d_gex = dbgout("gex", [NE, T], F32)

                def _(e):
                    sem.wait(e, "xnTc15")
                    sem.dma(e, "qD", d_xnT[:, :], xnT[:, :])
                on("gpsimd", _)

                def _(e):
                    sem.wait(e, "statsT")
                    sem.dma(e, "qD", d_statsT[:, 0:128], statsT[:, :])
                    sem.dma(e, "qD", d_statsT[:, 128:256], statsT2[:, :])
                on("gpsimd", _)

                def _(e):
                    sem.wait(e, "xcons15")
                    sem.dma(e, "qD", d_stats[:, :], stats[:, :])
                    sem.wait(e, "mubc3")
                    sem.dma(e, "qD", d_mubc[:, 0:T], mu_bc[:, :])
                    sem.dma(e, "qD", d_mubc[:, T:2 * T], rstd_bc[:, :])
                    sem.wait(e, "gates")
                    sem.dma(e, "qD", d_gex[:, :], gex[:, :])
                on("sync", _)
                d_probsT = dbgout("probsT", [NE, T], F32)
                d_lohi = dbgout("lohi", [NE, 2], F32)
                d_bis = dbgout("bis", [NE, 8], F32)
                d_idx = dbgout("idx", [NE * CAP, 1], I32)
                d_mask = dbgout("mask", [NE, T], F16)

                def _(e):
                    sem.wait(e, "xnTc7")
                    sem.wait(e, "routing_rows")
                    sem.dma(e, "qD", d_probsT[:, :], probsT[:, :])
                    sem.dma(e, "qD", d_bis[:, :], bisdbg[:, :])
                    sem.dma(e, "qD", d_lohi[:, 0:1], lo[:, :],
                            allow_slow_non_contiguous=True)
                    sem.dma(e, "qD", d_lohi[:, 1:2], hi[:, :],
                            allow_slow_non_contiguous=True)
                    sem.dma(e, "qD", d_mask[:, :], cmpw[:, :])
                    sem.wait(e, "idx_sb")
                    sem.dma(e, "qD", d_idx[:, :], idx_scr[:, :])
                on("sync", _)

            p1m5.close()
            p1m4.close()
            p1m3.close()
            fence("p1_end")
            p1.close()

        if PHASES >= 3:
            p2aw = ExitStack()
            Wq_bf = sb("Wq_bf", [128, 8 * C], BF16, p2aw)
            qA = sb("qA", [128, C], F32, p2aw)
            qB = sb("qB", [128, C], F32, p2aw)
            qC = sb("qC", [128, C], F32, p2aw)
            qD = sb("qD", [128, 16], F32, p2aw)
            qE = sb("qE", [128, 16], F32, p2aw)
            qF = sb("qF", [128, 16], F32, p2aw)
            qn_bf = sb("qn_bf", [128, C], BF16, p2aw)
            ps_q = ps("ps_q", [128, C], F32, p2aw)
            ps_trq = ps("ps_trq", [128, C], BF16, p2aw)

            def _(e):
                sem.dma(e, "qQO", Wq_bf[:].rearrange("p (kc a) -> p kc a", kc=8),
                        Wq_d.rearrange("(kc p) a -> p kc a", p=128), mname="ld_q")
            on("gpsimd", _)

            for u in range(16):
                def _(e, u=u):
                    if u == 0:
                        sem.wait(e, "ld_q")
                    else:
                        sem.wait(e, "qadd%d" % (u - 1))
                    for kc in range(8):
                        for nh in range(2):
                            inst = e.matmul(
                                ps_q[:, nh * 512:(nh + 1) * 512],
                                xnT[:, kc * T + u * 128: kc * T + u * 128 + 128],
                                Wq_bf[:, kc * C + nh * 512: kc * C + (nh + 1) * 512],
                                start=(kc == 0), stop=(kc == 7))
                    sem.incm(inst, "tS", "qmm%d" % u)
                    if u >= 1:
                        sem.wait(e, "qn%d" % (u - 1))
                        if u >= 2:
                            sem.wait(e, "qtrc%d" % (u - 2))
                        for fc in range(8):
                            inst = e.transpose(ps_trq[:, fc * 128:(fc + 1) * 128],
                                               qn_bf[:, fc * 128:(fc + 1) * 128],
                                               ident[:, :])
                        sem.incm(inst, "tS", "qtr%d" % (u - 1))
                on("tensor", _)

                def _(e, u=u):
                    sem.wait(e, "qmm%d" % u)
                    inst = e.tensor_add(qA[:, :], ps_q[:, :], BC(0))
                    sem.incm(inst, "vS", "qadd%d" % u)
                    e.wait_ge(sem.h["vS"], sem.c["vS"])
                    inst = e.tensor_scalar_min(qB[:, :], qA[:, :], 0.0)
                    sem.incm(inst, "vS", "qmin%d" % u)
                    if u >= 1:
                        sem.wait(e, "qtr%d" % (u - 1))
                        inst = e.tensor_copy(
                            qnT[:, :].rearrange("p (fc t) -> p fc t", fc=8)[:, :, (u - 1) * 128:u * 128],
                            ps_trq[:, :].rearrange("p (fc t) -> p fc t", t=128))
                        sem.incm(inst, "vS", "qtrc%d" % (u - 1))
                on("vector", _)

                def _(e, u=u):
                    sem.wait(e, "qmin%d" % u)
                    inst = e.activation(qC[:, :], qB[:, :], AF.Exp)
                    sem.incm(inst, "sS", "qexp%d" % u)
                on("scalar", _)

                def _(e, u=u):
                    sem.wait(e, "qexp%d" % u)
                    sem.vself(e, e.scalar_tensor_tensor(qB[:, :], qA[:, :], 0.0,
                                                        qC[:, :], ALU.max, ALU.add))
                    sem.vself(e, e.tensor_scalar_add(qB[:, :], qB[:, :], -1.0))
                    sem.vself(e, e.tensor_mul(qC[:, :], qB[:, :], qB[:, :]))
                    inst = e.tensor_reduce(qD[:, :],
                                           qC[:, :].rearrange("p (a b) -> p a b", b=64),
                                           AX.X, ALU.add)
                    sem.incm(inst, "vS", "qss%d" % u)
                on("vector", _)

                def _(e, u=u):
                    sem.wait(e, "qss%d" % u)
                    inst = e.activation(qE[:, :], qD[:, :], AF.Ln)
                    sem.incm(inst, "sS", "qln%d" % u)
                    inst = e.activation(qF[:, :], qE[:, :], AF.Exp, scale=-0.5)
                    sem.incm(inst, "sS", "qsrt%d" % u)
                on("scalar", _)

                def _(e, u=u):
                    sem.wait(e, "qsrt%d" % u)
                    if u >= 1:
                        sem.wait(e, "qtr%d" % (u - 1))
                    for h in range(16):
                        inst = e.tensor_scalar(qn_bf[:, h * 64:(h + 1) * 64],
                                               qB[:, h * 64:(h + 1) * 64],
                                               qF[:, h:h + 1], None, ALU.mult)
                    sem.incm(inst, "vS", "qn%d" % u)
                on("vector", _)

            def _(e):
                sem.wait(e, "qn15")
                sem.wait(e, "qtrc14")
                for fc in range(8):
                    inst = e.transpose(ps_trq[:, fc * 128:(fc + 1) * 128],
                                       qn_bf[:, fc * 128:(fc + 1) * 128], ident[:, :])
                sem.incm(inst, "tS", "qtr15")
            on("tensor", _)

            def _(e):
                sem.wait(e, "qtr15")
                inst = e.tensor_copy(
                    qnT[:, :].rearrange("p (fc t) -> p fc t", fc=8)[:, :, 15 * 128:16 * 128],
                    ps_trq[:, :].rearrange("p (fc t) -> p fc t", t=128))
                sem.incm(inst, "vS", "qtrc15")
            on("vector", _)

            pR2 = ExitStack()
            mlo = sb("mlo", [NE, T], F16, pR2)
            scanw = sb("scanw", [NE, T], F16, pR2)
            srcrow = sb("srcrow", [NE, T], F16, pR2)
            dstrow = sb("dstrow", [NE, T], F16, pR2)
            band = dstrow
            iota8s = sb("iota8_s", [NE, T], F16, pR2)
            if DEBUG >= 1:
                bisdbg = sb("bisdbg", [NE, 8], F32, pR2)

            def _(e):
                sem.wait(e, "probsT")
                sem.wait(e, "probs_st")
                sem.dma(e, "qS", iota8s[:], iota8_d[:, :], mname="ld_iota")
            on("sync", _)

            def emit_bis(e, kk):
                V = lambda inst: sem.vself(e, inst)
                if kk == 0:
                    V(e.memset(lo[:, :], 0.0))
                    V(e.memset(hi[:, :], 1.0))
                    sem.wait(e, "probsT")
                    sem.wait(e, "probs_st")
                    iters = range(0, 2)
                elif kk <= 12:
                    iters = range(2 * kk, 2 * kk + 2)
                else:
                    iters = range(0, 0)
                for it in iters:
                    V(e.tensor_add(mid[:, :], lo[:, :], hi[:, :]))
                    V(e.tensor_scalar_mul(mid[:, :], mid[:, :], 0.5))
                    V(e.tensor_scalar(cmpw[:, :], probsT[:, :], mid[:, 0:1], 0.0,
                                      ALU.is_gt, ALU.add, accum_out=cnt[:, :]))
                    V(e.tensor_scalar(bm[:, :], cnt[:, :], 512.0, None, ALU.is_gt))
                    V(e.tensor_copy(msku8[:, :], bm[:, :]))
                    V(e.select(lo[:, :], msku8[:, :], mid[:, :], lo[:, :]))
                    V(e.select(hi[:, :], msku8[:, :], hi[:, :], mid[:, :]))
                if kk == 13:
                    sem.wait(e, "ld_iota")
                    V(e.tensor_scalar(cmpw[:, :], probsT[:, :], hi[:, 0:1], 0.0,
                                      ALU.is_gt, ALU.add, accum_out=c1[:, :]))
                    V(e.tensor_scalar(mlo[:, :], probsT[:, :], lo[:, 0:1], None, ALU.is_gt))
                    V(e.tensor_sub(band[:, :], mlo[:, :], cmpw[:, :]))
                    V(e.tensor_tensor_scan(scanw[:, :], band[:, :], band[:, :], 0.0,
                                           ALU.add, ALU.bypass))
                    V(e.tensor_scalar(takeb[:, :], c1[:, :], -1.0, 512.0, ALU.mult, ALU.add))
                    V(e.tensor_scalar(mlo[:, :], scanw[:, :], takeb[:, 0:1], None, ALU.is_le))
                    V(e.tensor_mul(band[:, :], band[:, :], mlo[:, :]))
                    V(e.tensor_add(cmpw[:, :], cmpw[:, :], band[:, :]))
                    V(e.tensor_tensor_scan(scanw[:, :], cmpw[:, :], cmpw[:, :], 0.0,
                                           ALU.add, ALU.bypass))
                    V(e.tensor_scalar(scanw[:, :], scanw[:, :], -1.0, None, ALU.add))
                    V(e.tensor_mul(dstrow[:, :], scanw[:, :], cmpw[:, :]))
                    V(e.tensor_scalar(mlo[:, :], cmpw[:, :], -BIG, BIG, ALU.mult, ALU.add))
                    V(e.tensor_add(dstrow[:, :], dstrow[:, :], mlo[:, :]))
                    V(e.tensor_mul(srcrow[:, :], iota8s[:, :], cmpw[:, :]))
                    inst = e.tensor_add(srcrow[:, :], srcrow[:, :], mlo[:, :])
                    sem.incm(inst, "vS", "routing_rows")

            def emit_sdtr(e):
                sem.wait(e, "routing_rows")
                for cch in range(16):
                    e.transpose(ps_sd[:, cch * 8:(cch + 1) * 8],
                                srcrow[:, cch * 128:(cch + 1) * 128], identh[0:NE, 0:NE])
                    inst = e.transpose(ps_sd[:, 128 + cch * 8:128 + (cch + 1) * 8],
                                       dstrow[:, cch * 128:(cch + 1) * 128], identh[0:NE, 0:NE])
                sem.incm(inst, "tS", "sd_tr")

            def emit_sdicopy(e):
                sem.wait(e, "sd_tr")
                e.tensor_copy(sdf[:, :], ps_sd[:, :])
                inst = e.tensor_copy(sdi[:, :], sdf[:, :])
                sem.incm(inst, "vS", "sdi")

            def emit_compaction(e):
                sem.wait(e, "sdi")
                breg = e.alloc_register("bnd")
                e.reg_mov(breg, CAP - 1)
                for ex in range(NE):
                    for cch in range(16):
                        inst = e.indirect_dma_start(
                            out=idx_scr[:, :],
                            out_offset=IndirectOffsetOnAxis(
                                sdi[:, 128 + cch * 8 + ex: 128 + cch * 8 + ex + 1], 0),
                            in_=sdi[:, cch * 8 + ex: cch * 8 + ex + 1],
                            in_offset=None,
                            element_offset=ex * CAP,
                            bounds_check=breg,
                            oob_is_err=False,
                        )
                        sem.inc(inst, "qG", 16)
                e.free_register(breg)
                sem.mark("idx_done", "qG")
                sem.waitq(e, "qG")
                inst = e.dma_start(
                    out=idx_sb[:, :].rearrange("p (ex j) -> p ex j", j=4),
                    in_=idx_scr[:, 0].rearrange("(ex p j) -> p ex j", ex=NE, p=128))
                sem.inc(inst, "qG", 16)
                sem.mark("idx_sb", "qG")

            pR2.close()
            p2aw.close()
            pR.close()
            pA.close()

            DO_P2B = PHASES >= 4
            p2b = ExitStack()
            Wo_bf = sb("Wo_bf", [128, 8 * C], BF16, p2b)
            expA = [sb("expA0", [128, 16 * 512], BF16, p2b)]
            bias_bc2 = sb("bias_bc2", [128, 3 * C], F32, p2b)
            BCX[0] = bias_bc2
            expA.append(sb("expA1", [128, 16 * 512], BF16, p2b))
            ctxT = sb("ctxT", [128, 8 * 512], BF16, p2b)
            rbv = sb("rbv", [1, 512], F32, p2b)
            rbc = sb("rbc", [64, 512], F32, p2b)
            rb_bf = sb("rb_bf", [1, 512], BF16, p2b)
            ones_bf1 = sb("ones_bf1", [1, 128], BF16, p2b)
            xres = sb("xres", [128, 4 * C], BF16, p2b)
            sA = sb("sA", [128, C], F32, p2b)
            sB = sb("sB", [128, C], F32, p2b)
            s_out = sb("s_out", [128, C], F32, p2b)
            st1 = sb("st1", [128, 1], F32, p2b)
            st2 = sb("st2", [128, 1], F32, p2b)
            st3 = sb("st3", [128, 1], F32, p2b)
            sdf = sb("sdf", [128, 256], F16, p2b)
            sdi = sb("sdi", [128, 256], I32, p2b)
            ps_s = ps("ps_s", [128, 4 * 512], F32, p2b)
            ps_c = [ps("ps_c%d" % i, [128, 512], F32, p2b) for i in range(2)]
            ps_rb = ps("ps_rb", [64, 512], F32, p2b)
            ps_sd = ps("ps_sd", [128, 256], F16, p2b)
            ps_o = ps_s

            def _(e):
                inst = e.memset(ones_bf1[:, :], 1.0)
                sem.incm(inst, "vS", "ones_bf1")
            on("vector", _)

            def _(e):
                sem.wait(e, "qmm15")
                sem.dma(e, "qQO", Wo_bf[:].rearrange("p (kc a) -> p kc a", kc=8),
                        Wo_d.rearrange("(kc p) a -> p kc a", p=128), mname="ld_o")
                sem.wait(e, "routing_rows")
                for jj, j in enumerate((3, 4, 5)):
                    sem.dma(e, "qS", BC(jj + 1),
                            biases_d[0:1, j * C:(j + 1) * C].broadcast_to([128, C]),
                            mname="ld_bc2" if jj == 2 else None)
                for ex in range(NE):
                    sem.dma(e, "qWc", w1bf[ex, :, :], w1_d[ex, :, :])
                    sem.dma(e, "qWc", w2bf[ex, :, :], w2_d[ex, :, :])
                sem.mark("wcast", "qWc")
            on("gpsimd", _)

            on("gpsimd", emit_compaction)

            for ti in range(4 if DO_P2B else 0):
                def _(e, ti=ti):
                    if ti >= 1:
                        sem.wait(e, "x2a%d_3" % (ti - 1))
                    else:
                        sem.wait(e, "routing_rows")
                    sem.dma(e, "qR", xres[:, :].rearrange("p (sub c) -> p sub c", c=C),
                            xn_scr[ti * 512:(ti + 1) * 512, :].rearrange(
                                "(sub p) c -> p sub c", p=128),
                            mname="xres%d" % ti)
                on("sync", _)

                def pk(ti, h):
                    if h >= 0:
                        return (ti, h)
                    return (ti - 1, 16 + h)

                for k in range(18):
                    def _(e, ti=ti, k=k):
                        if 2 <= k:
                            h2 = k - 2
                            sem.wait(e, "crc%d_%d" % (ti, h2))
                            if ti == 0 and h2 == 0:
                                sem.wait(e, "ones_bf1")
                            p = pk(ti, h2 - 1)
                            if p[0] >= 0:
                                sem.wait(e, "cevac%d_%d" % p)
                            inst = e.matmul(ps_rb[:, :], ones_bf1[0:1, 0:64],
                                            rb_bf[0:1, :], start=True, stop=True)
                            sem.incm(inst, "tS", "crb%d_%d" % (ti, h2))
                        if 1 <= k <= 16:
                            h1 = k - 1
                            p = pk(ti, h1 - 2)
                            if p[0] >= 0:
                                sem.wait(e, "cevac%d_%d" % p)
                            for sc in range(16):
                                if sc % 4 == 0:
                                    sem.wait(e, "sexp%d_%d_%d" % (ti, h1, min(sc // 2 + 1, 7)))
                                inst = e.matmul(
                                    ps_c[h1 % 2][0:65, :],
                                    v_tm[:, sc * 1040 + h1 * 65: sc * 1040 + (h1 + 1) * 65],
                                    expA[h1 % 2][:, sc * 512:(sc + 1) * 512],
                                    start=(sc == 0), stop=(sc == 15))
                            sem.incm(inst, "tS", "cmm%d_%d" % (ti, h1))
                        if k <= 15:
                            h = k
                            fc = h // 2
                            hp = (h % 2) * 64
                            if ti == 0 and h == 0:
                                sem.wait(e, "qtrc15")
                                sem.wait(e, "ktrc15")
                                sem.wait(e, "vcp15")
                            if h == 0 and ti >= 1:
                                sem.wait(e, "x2a%d_3" % (ti - 1))
                            for g in range(16):
                                if g == 0:
                                    p = pk(ti, h - 1)
                                    if p[0] >= 0:
                                        sem.wait(e, "sexp%d_%d_%d" % (p[0], p[1], 7))
                                elif g % 4 == 0:
                                    sem.wait(e, "sexp%d_%d_%d" % (ti, h, g // 2 - 1))
                                inst = e.matmul(
                                    ps_s[:, (g % 4) * 512:(g % 4 + 1) * 512],
                                    knT[hp:hp + 64,
                                        fc * S + g * 128: fc * S + (g + 1) * 128],
                                    qnT[hp:hp + 64,
                                        fc * T + ti * 512: fc * T + ti * 512 + 512],
                                    start=True, stop=True)
                                sem.incm(inst, "tS", "smm%d_%d_%d" % (ti, h, g))
                        if ti == 0 and k == 15:
                            emit_sdtr(e)
                    on("tensor", _)

                    def _(e, ti=ti, k=k):
                        if k <= 15:
                            h = k
                            p = pk(ti, h - 2)
                            if p[0] >= 0:
                                sem.wait(e, "cmm%d_%d" % p)
                            for gp in range(8):
                                sem.wait(e, "smm%d_%d_%d" % (ti, h, 2 * gp + 1))
                                inst = e.activation(
                                    expA[h % 2][:, gp * 1024:(gp + 1) * 1024],
                                    ps_s[:, (2 * gp % 4) * 512:(2 * gp % 4 + 2) * 512],
                                    AF.Exp, scale=0.125)
                                sem.incm(inst, "sS", "sexp%d_%d_%d" % (ti, h, gp))
                        if 1 <= k <= 16:
                            h1 = k - 1
                            sem.wait(e, "cmm%d_%d" % (ti, h1))
                            p = pk(ti, h1 - 1)
                            if p[0] >= 0:
                                sem.wait(e, "crb%d_%d" % p)
                            inst = e.activation(rbv[0:1, :], ps_c[h1 % 2][64:65, :], AF.Ln)
                            sem.incm(inst, "sS", "cln%d_%d" % (ti, h1))
                            inst = e.activation(rb_bf[0:1, :], rbv[0:1, :], AF.Exp,
                                                scale=-1.0)
                            sem.incm(inst, "sS", "crc%d_%d" % (ti, h1))
                    on("scalar", _)

                    def _(e, ti=ti, k=k):
                        if 2 <= k:
                            h2 = k - 2
                            fc = h2 // 2
                            hp = (h2 % 2) * 64
                            sem.wait(e, "crb%d_%d" % (ti, h2))
                            sem.vself(e, e.tensor_copy(rbc[:, :], ps_rb[:, :]))
                            inst = e.tensor_mul(ctxT[hp:hp + 64, fc * 512:(fc + 1) * 512],
                                                ps_c[h2 % 2][0:64, :], rbc[:, :])
                            sem.incm(inst, "vS", "cevac%d_%d" % (ti, h2))
                        if ti == 0 and k <= 13:
                            emit_bis(e, k)
                        if ti == 0 and k == 16:
                            emit_sdicopy(e)
                    on("vector", _)

                for sub in range(4):
                    def _(e, ti=ti, sub=sub):
                        if sub == 0:
                            sem.wait(e, "cevac%d_15" % ti)
                            sem.wait(e, "ld_o")
                            if ti >= 1:
                                sem.wait(e, "x2a%d_3" % (ti - 1))
                        else:
                            sem.wait(e, "x2a%d_%d" % (ti, sub - 1))
                        for fc in range(8):
                            for nh in range(2):
                                inst = e.matmul(
                                    ps_o[:, nh * 512:(nh + 1) * 512],
                                    ctxT[:, fc * 512 + sub * 128: fc * 512 + sub * 128 + 128],
                                    Wo_bf[:, fc * C + nh * 512: fc * C + (nh + 1) * 512],
                                    start=(fc == 0), stop=(fc == 7))
                        sem.incm(inst, "tS", "omm%d_%d" % (ti, sub))
                    on("tensor", _)

                    def _(e, ti=ti, sub=sub):
                        sem.wait(e, "omm%d_%d" % (ti, sub))
                        sem.wait(e, "xres%d" % ti)
                        if ti == 0 and sub == 0:
                            sem.wait(e, "ld_bc2")
                            sem.wait(e, "sd_tr")
                        sem.vself(e, e.tensor_add(sA[:, :], ps_o[:, 0:C], BC(1)))
                        inst = e.tensor_add(sA[:, :], sA[:, :],
                                            xres[:, sub * C:(sub + 1) * C])
                        sem.incm(inst, "vS", "x2a%d_%d" % (ti, sub))
                        e.wait_ge(sem.h["vS"], sem.c["vS"])
                        sem.vself(e, e.tensor_reduce(st1[:, :], sA[:, :], AX.X, ALU.add))
                        sem.vself(e, e.tensor_scalar_mul(st1[:, :], st1[:, :], 1.0 / C))
                        sem.vself(e, e.tensor_mul(sB[:, :], sA[:, :], sA[:, :]))
                        sem.vself(e, e.tensor_reduce(st2[:, :], sB[:, :], AX.X, ALU.add))
                        sem.vself(e, e.tensor_scalar_mul(st2[:, :], st2[:, :], 1.0 / C))
                        sem.vself(e, e.tensor_mul(st3[:, :], st1[:, :], st1[:, :]))
                        inst = e.tensor_sub(st2[:, :], st2[:, :], st3[:, :])
                        sem.incm(inst, "vS", "ovar%d_%d" % (ti, sub))
                    on("vector", _)

                    def _(e, ti=ti, sub=sub):
                        sem.wait(e, "ovar%d_%d" % (ti, sub))
                        inst = e.activation(st3[:, :], st2[:, :], AF.Ln, bias=1e-12)
                        sem.incm(inst, "sS", "oln%d_%d" % (ti, sub))
                        inst = e.activation(st2[:, :], st3[:, :], AF.Exp, scale=-0.5)
                        sem.incm(inst, "sS", "osd%d_%d" % (ti, sub))
                    on("scalar", _)

                    def _(e, ti=ti, sub=sub):
                        sem.wait(e, "osd%d_%d" % (ti, sub))
                        if not (ti == 0 and sub == 0):
                            sem.wait(e, "sst%d_%d" % ((ti, sub - 1) if sub else (ti - 1, 3)))
                        sem.vself(e, e.tensor_scalar(sB[:, :], sA[:, :], st1[:, 0:1],
                                                     st2[:, 0:1], ALU.subtract, ALU.mult))
                        sem.vself(e, e.tensor_mul(sB[:, :], sB[:, :], BC(2)))
                        inst = e.tensor_add(s_out[:, :], sB[:, :], BC(3))
                        sem.incm(inst, "vS", "sfin%d_%d" % (ti, sub))
                    on("vector", _)

                    def _(e, ti=ti, sub=sub):
                        sem.wait(e, "sfin%d_%d" % (ti, sub))
                        sem.dma(e, "qSs",
                                s_scr[ti * 512 + sub * 128: ti * 512 + (sub + 1) * 128, :],
                                s_out[:, :], mname="sst%d_%d" % (ti, sub))
                    on("sync", _)

            if DEBUG >= 1 and DO_P2B:
                d_s = dbgout("s", [T, C], F32)

                def _(e):
                    sem.wait(e, "sst3_3")
                    sem.waitq(e, "qSs")
                    for i in range(16):
                        sem.dma(e, "qD", d_s[i * 128:(i + 1) * 128, :],
                                s_scr[i * 128:(i + 1) * 128, :])
                on("sync", _)

            fence("p2_end")
            p2b.close()
        if PHASES < 3:
            pA.close()
        pQ.close()
        pKV.close()

        if PHASES >= 6:
            p3 = ExitStack()
            w1sb = [sb("w1sb%d" % i, [128, 8 * 1024], BF16, p3) for i in range(2)]
            w2sb = sb("w2sb", [128, 32 * 1024], BF16, p3)
            xin_tm = sb("xin_tm", [128, 4 * C], BF16, p3)
            xinT = sb("xinT", [128, 8 * 512], BF16, p3)
            p_sb = sb("p_sb", [128, 4 * NE], F32, p3)
            ghT = sb("ghT", [128, 32 * 512], BF16, p3)
            y_sb = [sb("y_sb%d" % i, [128, C], F32, p3) for i in range(2)]
            ps_h = [ps("ps_h%d" % i, [128, 512], F32, p3) for i in range(2)]
            ps_y = [ps("ps_y%d" % i, [128, C], F32, p3) for i in range(2)]
            ps_trf = ps("ps_trf", [128, C], BF16, p3)

            def _(e):
                e.memset(y_sb[0][:, :], 0.0)
                inst = e.memset(y_sb[1][:, :], 0.0)
                sem.incm(inst, "vS", "yzero")
            on("vector", _)

            def _(e):
                sem.wait(e, "yzero")
                for i in range(16):
                    sem.dma(e, "qZ", g_scr[i * 128:(i + 1) * 128, :], y_sb[i % 2][:, :])
                sem.mark("gzero", "qZ")
            on("gpsimd", _)

            for ex in range(NE):
                def _(e, ex=ex):
                    if ex == 0:
                        sem.wait(e, "wcast")
                    for g in range(4):
                        gg = ex * 4 + g
                        if gg >= 2:
                            ex2, g2 = divmod(gg - 2, 4)
                            sem.wait(e, "l1g%d_%d" % (ex2, g2))
                        sem.dma(e, "qW%d" % (g % 2),
                                w1sb[g % 2][:, :].rearrange("p (kc m) -> p kc m", kc=8),
                                w1bf[ex, :, g * 1024:(g + 1) * 1024].rearrange(
                                    "(kc p) m -> p kc m", p=128),
                                mname="w1ld%d_%d" % (ex, g))
                    if ex >= 1:
                        sem.wait(e, "l2%d_3" % (ex - 1))
                    sem.dma(e, "qW2",
                            w2sb[:, :].rearrange("p (kc m) -> p kc m", kc=32),
                            w2bf[ex, :, :].rearrange("(kc p) m -> p kc m", p=128),
                            mname="w2ld%d" % ex)
                on("sync", _)

                def _(e, ex=ex):
                    if ex == 0:
                        sem.wait(e, "idx_sb")
                    else:
                        sem.wait(e, "xtrc%d" % (ex - 1))
                        sem.wait(e, "yev%d_3" % (ex - 1))
                    for j in range(4):
                        inst = e.indirect_dma_start(
                            out=xin_tm[:, j * C:(j + 1) * C], out_offset=None,
                            in_=xn_scr[:, :],
                            in_offset=IndirectOffsetOnAxis(
                                idx_sb[:, ex * 4 + j: ex * 4 + j + 1], 0),
                        )
                        sem.inc(inst, "qGa", 16)
                    for j in range(4):
                        inst = e.indirect_dma_start(
                            out=p_sb[:, j * NE:(j + 1) * NE], out_offset=None,
                            in_=probs_scr[:, :],
                            in_offset=IndirectOffsetOnAxis(
                                idx_sb[:, ex * 4 + j: ex * 4 + j + 1], 0),
                        )
                        sem.inc(inst, "qGa", 16)
                    sem.mark("gat%d" % ex, "qGa")
                on("gpsimd", _)

                def _(e, ex=ex):
                    sem.wait(e, "gat%d" % ex)
                    for j in range(4):
                        if j == 0:
                            if ex >= 1:
                                sem.wait(e, "xinTe%d_3" % (ex - 1))
                        else:
                            sem.wait(e, "xinTe%d_%d" % (ex, j - 1))
                        for fc in range(8):
                            inst = e.transpose(ps_trf[:, fc * 128:(fc + 1) * 128],
                                               xin_tm[:, j * C + fc * 128: j * C + (fc + 1) * 128],
                                               ident[:, :])
                        sem.incm(inst, "tS", "xtr%d_%d" % (ex, j))
                        if j == 3:
                            sem.mark("xtrc%d" % ex, "tS")
                    for mc in range(32):
                        g = mc // 8
                        if mc == 0:
                            sem.wait(e, "xinTe%d_3" % ex)
                        if mc % 8 == 0:
                            sem.wait(e, "w1ld%d_%d" % (ex, g))
                        gg = ex * 32 + mc
                        if gg >= 2:
                            ex2, mc2 = divmod(gg - 2, 32)
                            sem.wait(e, "gel%d_%d" % (ex2, mc2))
                        for kc in range(8):
                            inst = e.matmul(
                                ps_h[mc % 2][:, :],
                                w1sb[g % 2][:, kc * 1024 + (mc % 8) * 128:
                                            kc * 1024 + (mc % 8) * 128 + 128],
                                xinT[:, kc * 512:(kc + 1) * 512],
                                start=(kc == 0), stop=(kc == 7))
                        sem.incm(inst, "tS", "l1%d_%d" % (ex, mc))
                        if mc % 8 == 7:
                            sem.mark("l1g%d_%d" % (ex, g), "tS")
                    for mc2 in range(4):
                        gg = ex * 4 + mc2
                        if gg >= 2:
                            ex2, m2 = divmod(gg - 2, 4)
                            sem.wait(e, "yev%d_%d" % (ex2, m2))
                        for kc2 in range(32):
                            if kc2 == 0:
                                if mc2 == 0:
                                    sem.wait(e, "w2ld%d" % ex)
                                sem.wait(e, "gel%d_31" % ex)
                            for nh in range(2):
                                inst = e.matmul(
                                    ps_y[mc2 % 2][:, nh * 512:(nh + 1) * 512],
                                    ghT[:, kc2 * 512 + mc2 * 128: kc2 * 512 + mc2 * 128 + 128],
                                    w2sb[:, kc2 * 1024 + nh * 512:
                                         kc2 * 1024 + (nh + 1) * 512],
                                    start=(kc2 == 0), stop=(kc2 == 31))
                        sem.incm(inst, "tS", "l2%d_%d" % (ex, mc2))
                on("tensor", _)

                def _(e, ex=ex):
                    for j in range(4):
                        sem.wait(e, "xtr%d_%d" % (ex, j))
                        inst = e.tensor_copy(
                            xinT[:, :].rearrange("p (kc t) -> p kc t", kc=8)[:, :, j * 128:(j + 1) * 128],
                            ps_trf[:, :].rearrange("p (fc t) -> p fc t", t=128))
                        sem.incm(inst, "vS", "xinTe%d_%d" % (ex, j))
                on("vector", _)

                def _(e, ex=ex):
                    for mc in range(32):
                        sem.wait(e, "l1%d_%d" % (ex, mc))
                        inst = e.activation(ghT[:, mc * 512:(mc + 1) * 512],
                                            ps_h[mc % 2][:, :], AF.Gelu)
                        sem.incm(inst, "sS", "gel%d_%d" % (ex, mc))
                    for mc2 in range(4):
                        sem.wait(e, "l2%d_%d" % (ex, mc2))
                        gg = ex * 4 + mc2
                        if gg >= 2:
                            ex2, m2 = divmod(gg - 2, 4)
                            sem.wait(e, "sca%d_%d" % (ex2, m2))
                        if ex == 0 and mc2 == 0:
                            sem.wait(e, "gat0")
                        inst = e.activation(y_sb[mc2 % 2][:, :], ps_y[mc2 % 2][:, :],
                                            AF.Copy,
                                            scale=p_sb[:, mc2 * NE + ex: mc2 * NE + ex + 1])
                        sem.incm(inst, "sS", "yev%d_%d" % (ex, mc2))
                on("scalar", _)

                def _(e, ex=ex):
                    for mc2 in range(4):
                        sem.wait(e, "yev%d_%d" % (ex, mc2))
                        if ex == 0 and mc2 == 0:
                            sem.wait(e, "gzero")
                        inst = e.indirect_dma_start(
                            out=g_scr[:, :],
                            out_offset=IndirectOffsetOnAxis(
                                idx_sb[:, ex * 4 + mc2: ex * 4 + mc2 + 1], 0),
                            in_=y_sb[mc2 % 2][:, :],
                            in_offset=None,
                            compute_op=ALU.add,
                        )
                        sem.inc(inst, "qS%d" % (mc2 % 2), 16)
                        sem.mark("sca%d_%d" % (ex, mc2), "qS%d" % (mc2 % 2))
                    sem.waitq(e, "qS0")
                    sem.waitq(e, "qS1")
                on("gpsimd", _)

            if DEBUG >= 1:
                d_g = dbgout("g", [T, C], F32)

                def _(e):
                    sem.waitq(e, "qS0")
                    sem.waitq(e, "qS1")
                    for i in range(16):
                        sem.dma(e, "qD", d_g[i * 128:(i + 1) * 128, :],
                                g_scr[i * 128:(i + 1) * 128, :])
                on("sync", _)

            fence("p3_end")
            p3.close()

        if PHASES >= 7:
            p4 = ExitStack()
            gt = [sb("gt%d" % i, [128, 2 * C], F32, p4) for i in range(3)]
            xt4 = [sb("xt4%d" % i, [128, 2 * C], F32, p4) for i in range(3)]
            st4 = [sb("st4%d" % i, [128, 2 * C], F32, p4) for i in range(3)]
            sg4 = sb("sg4", [128, 2 * C], F32, p4)
            o4 = sb("o4", [128, 2 * C], F32, p4)

            for i in range(8):
                def _(e, i=i):
                    if i >= 3:
                        sem.wait(e, "cmb%d" % (i - 3))
                    qc = "qC%d" % (i % 3)
                    sem.dma(e, qc, gt[i % 3][:, :].rearrange("p (j c) -> p j c", c=C),
                            g_scr[i * 256:(i + 1) * 256, :].rearrange("(j p) c -> p j c", p=128))
                    sem.dma(e, qc, xt4[i % 3][:, :].rearrange("p (j c) -> p j c", c=C),
                            x_tm[i * 256:(i + 1) * 256, :].rearrange("(j p) c -> p j c", p=128))
                    sem.dma(e, qc, st4[i % 3][:, :].rearrange("p (j c) -> p j c", c=C),
                            s_scr[i * 256:(i + 1) * 256, :].rearrange("(j p) c -> p j c", p=128),
                            mname="cld%d" % i)
                on("sync", _)

                def _(e, i=i):
                    sem.wait(e, "cld%d" % i)
                    if i >= 1:
                        sem.wait(e, "cmb%d" % (i - 1))
                    inst = e.activation(sg4[:, :], gt[i % 3][:, :], AF.Silu)
                    sem.incm(inst, "sS", "sil%d" % i)
                on("scalar", _)

                def _(e, i=i):
                    sem.wait(e, "sil%d" % i)
                    if i >= 1:
                        sem.wait(e, "yout%d" % (i - 1))
                    sem.vself(e, e.tensor_mul(sg4[:, :], sg4[:, :], st4[i % 3][:, :]))
                    inst = e.tensor_add(o4[:, :], sg4[:, :], xt4[i % 3][:, :])
                    sem.incm(inst, "vS", "cmb%d" % i)
                on("vector", _)

                def _(e, i=i):
                    sem.wait(e, "cmb%d" % i)
                    sem.dma(e, "qY",
                            y_d[i * 256:(i + 1) * 256, :].rearrange("(j p) c -> p j c", p=128),
                            o4[:, :].rearrange("p (j c) -> p j c", c=C),
                            mname="yout%d" % i)
                on("sync", _)
            p4.close()

        def _(e):
            for smn in list(sem.h.keys()):
                if smn.startswith("q"):
                    sem.waitq(e, smn)
        on("sync", _)

    gctx.close()
    return nc, dbg, sem.marks


def make_common(inp):
    m = {}
    m["srcT"] = np.ascontiguousarray(inp["L0_attn_source_embedding"].T)
    m["valT"] = np.ascontiguousarray(inp["L0_attn_value_embedding"].T)
    m["Wq"] = np.ascontiguousarray(inp["L0_attn_Wq"])
    m["Wk"] = np.ascontiguousarray(inp["L0_attn_Wk"])
    m["Wv"] = np.ascontiguousarray(inp["L0_attn_Wv"])
    m["Wo"] = np.ascontiguousarray(inp["L0_attn_Wo"])
    m["biases"] = np.ascontiguousarray(np.concatenate([
        inp["L0_attn_bq"], inp["L0_attn_bk"], inp["L0_attn_bv"], inp["L0_attn_bo"],
        inp["L0_attn_ln_g"], inp["L0_attn_ln_b"]]).astype(np.float32)).reshape(1, 6 * C)
    m["gate_w"] = np.ascontiguousarray(inp["L0_ffn_gate_w"])
    m["w1"] = np.ascontiguousarray(inp["L0_ffn_w1"])
    m["w2"] = np.ascontiguousarray(inp["L0_ffn_w2"])
    m["ident"] = np.eye(128).astype(ml_dtypes.bfloat16)
    m["onesf"] = np.ones((1, 128), np.float32)
    m["onescf"] = np.ones((128, 1), np.float32)
    sel = np.zeros((16, 16 * 128), np.float32)
    for i in range(16):
        sel[i, i * 128:(i + 1) * 128] = 1.0
    m["sel16"] = sel.astype(ml_dtypes.bfloat16)
    m["iota8"] = np.broadcast_to(np.arange(T, dtype=np.float16), (NE, T)).copy()
    cst = np.zeros((128, 8), np.float32)
    cst[:, 0] = -1.0
    cst[:, 1] = 1e-5
    cst[:, 2] = 1e-12
    cst[:, 3] = 0.125
    cst[:, 4] = -0.5
    m["consts"] = cst
    return m


def kernel(**inputs):
    _nc1, _dbg1, marks = build_nc()
    nc, dbg, _ = build_nc(future=marks)
    common = make_common(inputs)
    x = inputs["L0_attn_target_embedding"]
    in_maps = []
    n_cores = int(os.environ.get("DEV_CORES", "8"))
    for b in range(n_cores):
        mm = dict(common)
        mm["x_tm"] = np.ascontiguousarray(x[b])
        mm["xT"] = np.ascontiguousarray(x[b].T)
        in_maps.append(mm)
    res = run_bass_kernel_spmd(nc, in_maps, core_ids=list(range(n_cores)))
    if DEBUG:
        return res
    return np.stack([r["y"] for r in res.results], 0)
